# version 05
# baseline (speedup 1.0000x reference)
import numpy as np
import ml_dtypes
import concourse.bass as bass
import concourse.mybir as mybir

F32 = mybir.dt.float32
BF16 = mybir.dt.bfloat16
AF = mybir.ActivationFunctionType
ALU = mybir.AluOpType
AX = mybir.AxisListType

D = 2048
T = 2048
KC = D // 128
MT = T // 128
NEG = -1e30
EPS_LN = 1e-5

ENGINES = ("tensor", "vector", "scalar", "gpsimd", "sync")


class Sem:
    __slots__ = ("h", "n", "name")

    def __init__(self, nc, name):
        self.h = nc.alloc_semaphore(name)
        self.n = 0
        self.name = name


class Em:

    def __init__(self, nc):
        self.nc = nc
        self.streams = {e: [] for e in ENGINES}
        self.sems = {}

    def sem(self, name):
        if name not in self.sems:
            self.sems[name] = Sem(self.nc, name)
        return self.sems[name]

    def emit(self, eng, fn, wait=(), inc=None, inc_k=1):
        waits = []
        for w in wait:
            if isinstance(w, Sem):
                waits.append((w, w.n))
            else:
                waits.append((w[0], w[1]))
        tok = None
        if inc is not None:
            inc.n += inc_k
            tok = (inc, inc.n)
        self.streams[eng].append((fn, waits, inc, inc_k))
        return tok

    def dma(self, eng, out_ap, in_ap, sem, wait=(), **kw):
        return self.emit(eng, lambda e: e.dma_start(out_ap, in_ap, **kw),
                         wait=wait, inc=sem, inc_k=16)

    def check(self):
        pos = {e: 0 for e in ENGINES}
        counts = {}
        progressed = True
        total = sum(len(s) for s in self.streams.values())
        done = 0
        while progressed:
            progressed = False
            for e in ENGINES:
                while pos[e] < len(self.streams[e]):
                    fn, waits, inc, inc_k = self.streams[e][pos[e]]
                    if all(counts.get(s.name, 0) >= v for s, v in waits):
                        if inc is not None:
                            counts[inc.name] = counts.get(inc.name, 0) + inc_k
                        pos[e] += 1
                        done += 1
                        progressed = True
                    else:
                        break
        if done != total:
            for e in ENGINES:
                if pos[e] < len(self.streams[e]):
                    fn, waits, inc, inc_k = self.streams[e][pos[e]]
                    missing = [(s.name, v, counts.get(s.name, 0)) for s, v in waits
                               if counts.get(s.name, 0) < v]
                    print(f"DEADLOCK {e}@{pos[e]}/{len(self.streams[e])}: needs {missing}")
            raise RuntimeError(f"deadlock: {done}/{total} instructions schedulable")
        print(f"schedulability ok: {total} instructions")

    def replay(self):
        nc = self.nc
        with nc.Block() as blk:
            for eng_name in ENGINES:
                stream = self.streams[eng_name]

                def make(stream):
                    def run(eng):
                        hi = {}
                        for fn, waits, inc, inc_k in stream:
                            byname = {}
                            for s, v in waits:
                                if v > 0 and (s.name not in byname or byname[s.name][1] < v):
                                    byname[s.name] = (s, v)
                            for s, v in byname.values():
                                if hi.get(s.name, 0) < v:
                                    eng.wait_ge(s.h, v)
                                    hi[s.name] = v
                            ins = fn(eng)
                            if inc is not None:
                                ins.then_inc(inc.h, inc_k)
                    return run

                getattr(blk, eng_name)(make(stream))


def bfbits(a):
    return np.ascontiguousarray(a.astype(ml_dtypes.bfloat16)).view(np.uint16)



def l0_rope_tables(scale_q):
    j = np.arange(32, dtype=np.float64)
    freqs = 10000.0 ** (-2.0 * j / 64.0)
    pos = np.arange(1, T + 1, dtype=np.float64)[None, :]
    th = freqs[:, None] * pos
    c = np.cos(th)
    s = np.sin(th)
    CC = np.concatenate([c, c, c, c], axis=0)
    SS = np.concatenate([-s, s, -s, s], axis=0)
    return (CC * scale_q).astype(np.float32), (SS * scale_q).astype(np.float32)


def l1_rope_tables(scale_q):
    k = np.arange(64, dtype=np.float64)
    freqs = 10000.0 ** (-2.0 * k / 128.0)
    pos = np.arange(0, T, dtype=np.float64)[None, :]
    th = freqs[:, None] * pos
    c = np.cos(th)
    s = np.sin(th)
    even = (np.arange(64) % 2 == 0)[:, None]
    A = np.where(even, c, s)
    B = np.where(even, -s, c)
    A2 = np.concatenate([A, A], axis=0)
    B2 = np.concatenate([B, B], axis=0)
    return (A2 * scale_q).astype(np.float32), (B2 * scale_q).astype(np.float32)


def l0_perm_matrix():
    P = np.zeros((128, 128), dtype=np.float32)
    for b in range(2):
        for i in range(32):
            P[b * 64 + i, b * 64 + 32 + i] = 1.0
            P[b * 64 + 32 + i, b * 64 + i] = 1.0
    return P


def l1_head_perm():
    perm = np.zeros(D, dtype=np.int64)
    for h in range(16):
        base = h * 128
        perm[base:base + 64] = base + np.arange(0, 128, 2)
        perm[base + 64:base + 128] = base + np.arange(1, 128, 2)
    return perm


def causal_mask_tile():
    return np.where(np.arange(128)[None, :] <= np.arange(128)[:, None],
                    0.0, NEG).astype(np.float32)


PARTBYTES = 229376
B0 = 17536
R1_OFF = B0
R2_OFF = B0 + 65536
CC_OFF = B0 + 131072
TS_OFF = B0 + 155648
WW_OFF = B0 + 172032
SM_OFF = B0 + 204800
FWC_OFF = CC_OFF
FWA_OFF = CC_OFF + 46080
FWB_OFF = FWA_OFF + 12288

SQ0 = 1.0 / np.sqrt(64.0)
SQ1 = 2.0 / np.sqrt(128.0)
GROUP_PHASES = {"A": (0, 1, 2), "B": (3,), "C": (4, 5, 6), "D": (7,),
                "P0": (0,), "P01": (0, 1), "P4": (4,), "ALL": (0, 1, 2, 3, 4, 5, 6, 7)}


class MK:
    def __init__(self, group="ALL"):
        self.group = group
        self.phases = GROUP_PHASES[group]
        nc = bass.Bass("TRN2", target_bir_lowering=False, debug=False)
        self.nc = nc
        self.em = Em(nc)
        self.ext_in = []
        self.ext_out = []
        self._mk_dram()
        self._mk_sbuf()
        self._mk_psum()
        self._mk_sems()

    def _mk_dram(self):
        nc = self.nc
        g = self.group

        def ext(name, shape, dt):
            t = nc.dram_tensor(name, shape, dt, kind="ExternalInput")
            self.ext_in.append(name)
            return t.ap()

        def extout(name, shape, dt):
            t = nc.dram_tensor(name, shape, dt, kind="ExternalOutput")
            self.ext_out.append(name)
            return t.ap()

        def internal(name, shape, dt):
            return nc.dram_tensor(name, shape, dt, kind="Internal").ap()

        def bound(name, shape, dt, produced, consumed):
            if g == "ALL":
                return internal(name, shape, dt)
            if g == produced or (produced == "A" and g in ("P0", "P01")):
                return extout(name, shape, dt)
            if g == consumed or (consumed == "C" and g == "P4"):
                return ext(name, shape, dt)
            return None

        d = self.d = {}
        ing = {"A": ["x", "wq0", "bq0", "wk0", "bk0", "wv0", "bv0", "wo0", "bo0", "gw0"],
               "P0": ["x", "wq0", "bq0", "wk0", "bk0", "wv0", "bv0"],
               "P01": ["x", "wq0", "bq0", "wk0", "bk0", "wv0", "bv0"],
               "P4": ["wq1", "bq1", "wk1", "bk1", "wv1", "bv1", "rmsw"],
               "B": ["wg0", "wu0", "wd0", "swg0", "swu0", "swd0"],
               "C": ["wq1", "bq1", "wk1", "bk1", "wv1", "bv1", "wo1", "bo1", "gw1", "rmsw"],
               "D": ["w1", "w2", "sw1", "sw2"]}
        shapes = {
            "x": ([T, D], F32), "wq0": ([D, D], F32), "bq0": ([D], F32),
            "wk0": ([D, 512], F32), "bk0": ([512], F32), "wv0": ([D, 512], F32),
            "bv0": ([512], F32), "wo0": ([D, D], F32), "bo0": ([D], F32),
            "gw0": ([D, 8], F32),
            "wg0": ([8, D, 1408], F32), "wu0": ([8, D, 1408], F32), "wd0": ([8, 1408, D], F32),
            "swg0": ([D, 2816], F32), "swu0": ([D, 2816], F32), "swd0": ([2816, D], F32),
            "wq1": ([D, D], F32), "bq1": ([D], F32), "wk1": ([D, D], F32),
            "bk1": ([D], F32), "wv1": ([D, D], F32), "bv1": ([D], F32),
            "wo1": ([D, D], F32), "bo1": ([D], F32), "gw1": ([D, 16], F32),
            "rmsw": ([128, 16], F32),
            "w1": ([16, D, 1408], F32), "w2": ([16, 1408, D], F32),
            "sw1": ([D, 1408], F32), "sw2": ([1408, D], F32),
        }
        for grp in ("A", "B", "C", "D"):
            for n in ing[grp]:
                if g == "ALL" or g == grp:
                    d[n] = ext(n, *shapes[n])
        if g in ("P0", "P01", "P4"):
            for n in ing[g]:
                if n not in d:
                    d[n] = ext(n, *shapes[n])
        d["xA"] = bound("xA", [T, D], F32, "A", "B")
        d["xnA"] = bound("xnA", [D, T], BF16, "A", "B")
        d["cmbA"] = bound("cmbA", [128, MT * 10], F32, "A", "B")
        d["xB"] = bound("xB", [T, D], F32, "B", "C")
        d["xC"] = bound("xC", [T, D], F32, "C", "D")
        d["xnC"] = bound("xnC", [D, T], BF16, "C", "D")
        d["cmbC"] = bound("cmbC", [128, MT * 17], F32, "C", "D")
        if g in ("ALL", "D"):
            d["out"] = extout("out", [T, D], F32)
        if g in ("ALL", "A", "C", "P0", "P01", "P4"):
            d["k_hbm"] = internal("k_hbm", [512, T], BF16)
            d["v_hbm"] = internal("v_hbm", [T, D], BF16)
            d["y_hbm"] = internal("y_hbm", [T, D], BF16)
        if g in ("ALL", "C", "P4"):
            d["r_hbm"] = internal("r_hbm", [T, D], F32)

        self.ident_t = nc.inline_tensor(bfbits(np.eye(128, dtype=np.float32)), name="c_ident")
        self.ones_t = nc.inline_tensor(bfbits(np.ones((1, 512), dtype=np.float32)), name="c_ones")
        self.perm_t = nc.inline_tensor(bfbits(l0_perm_matrix()), name="c_perm")
        P1m = np.zeros((128, 128), dtype=np.float32)
        for i in range(64):
            P1m[i, 64 + i] = 1.0
            P1m[64 + i, i] = 1.0
        self.perm1_t = nc.inline_tensor(bfbits(P1m), name="c_perm1")
        self.mask_t = nc.inline_tensor(causal_mask_tile(), name="c_mask")
        cc0, ss0 = l0_rope_tables(SQ0)
        cc0u, ss0u = l0_rope_tables(1.0)
        a1, b1 = l1_rope_tables(SQ1)
        a1u, b1u = l1_rope_tables(1.0)
        self.tbl0_t = nc.inline_tensor(np.concatenate(
            [bfbits(cc0), bfbits(ss0), bfbits(cc0u), bfbits(ss0u)], axis=1), name="c_tbl0")
        self.tbl1_t = nc.inline_tensor(np.concatenate(
            [bfbits(a1), bfbits(b1), bfbits(a1u), bfbits(b1u)], axis=1), name="c_tbl1")

    def _mk_sbuf(self):
        nc = self.nc
        at = nc.alloc_sbuf_tensor_at
        self.R1 = at("R1", [128, 32768], BF16, offset=R1_OFF)
        self.R2 = at("R2", [128, 32768], BF16, offset=R2_OFF)
        self.ACC = at("ACCv", [128, 16384], F32, offset=R2_OFF)
        self.CCR = at("CCR", [128, 12288], BF16, offset=CC_OFF)
        self.WA = at("WAb", [128, 8192], BF16, offset=WW_OFF)
        self.WB = at("WBb", [128, 8192], BF16, offset=WW_OFF + 16384)
        self.SCF = [at("SCFa", [128, 2048], F32, offset=WW_OFF),
                    at("SCFb", [128, 2048], F32, offset=WW_OFF + 16384)]
        self.TSa = at("TSa", [128, 2048], F32, offset=TS_OFF)
        self.TSb = at("TSb", [128, 2048], F32, offset=TS_OFF + 8192)
        self.TMPF = at("TMPF", [128, 1024], F32, offset=TS_OFF + 8192)
        self.WC = at("WCb", [128, 23040], BF16, offset=FWC_OFF)
        self.FWA = at("FWAb", [128, 6144], BF16, offset=FWA_OFF)
        self.FWB = at("FWBb", [128, 6144], BF16, offset=FWB_OFF)
        o = SM_OFF

        def sm(name, shape, dt, nbytes):
            nonlocal o
            t = at(name, shape, dt, offset=o)
            o += nbytes
            return t
        self.ST = sm("ST", [128, 96], F32, 384)
        self.CMB = sm("CMB", [128, MT * 18], F32, MT * 18 * 4)
        self.IDT = sm("IDT", [128, 128], BF16, 256)
        self.PERM = sm("PERM", [128, 128], BF16, 256)
        self.PERM1 = sm("PERM1", [128, 128], BF16, 256)
        self.ONESB = sm("ONESB", [128, 512], BF16, 1024)
        self.MASK = sm("MASK", [128, 128], F32, 512)
        self.RMW = sm("RMW", [128, 16], F32, 64)
        self.BROW = sm("BROW", [128, 1536], BF16, 3072)
        assert o <= PARTBYTES, f"SM overflow {o}"

    def _mk_psum(self):
        nc = self.nc
        self.PS = nc.alloc_psum_tensor("PS", [128, 2048], F32)
        self.PA = nc.alloc_psum_tensor("PA", [128, 512], F32)
        self.PBb = nc.alloc_psum_tensor("PBb", [128, 1024], BF16)
        self.PC = nc.alloc_psum_tensor("PC", [128, 512], F32)
        self.PDB = nc.alloc_psum_tensor("PDB", [128, 1024], BF16)

    def _mk_sems(self):
        s = self.em.sem
        self.pe = s("pe")
        self.ve = s("ve")
        self.se = s("se")
        self.dW = [s("dW0"), s("dW1"), s("dW2")]
        self.dX = [s("dX0"), s("dX1")]
        self.dS = [s("dS0"), s("dS1")]
        self.dR1 = s("dR1")
        self.dR2 = s("dR2")
        self.dST = s("dST")
        self.dB = [s("dB0"), s("dB1"), s("dB2")]
        self.dOUT = s("dOUT")

    def barrier(self, extra=()):
        em = self.em
        sems = [self.pe, self.ve, self.se, self.dR1, self.dR2, self.dST, self.dOUT] + \
               self.dW + self.dX + self.dS + self.dB + list(extra)
        toks = [(s, s.n) for s in sems if s.n > 0]
        for eng in ENGINES:
            em.emit(eng, lambda e: e.nop(), wait=toks)


    def ln_tile(self, x_slot, t_x, st_base, out_slot, rms=False, ln_f32_out=None):
        em = self.em
        ST = self.ST
        c = lambda i: ST[:, st_base + i:st_base + i + 1]
        t_sum = em.emit("vector", lambda v: v.reduce_sum(c(0), x_slot[:, :], axis=AX.X, negate=True),
                        wait=[t_x], inc=self.ve)
        t_sq = em.emit("scalar", lambda s: s.activation(out_slot[:, :], x_slot[:, :], AF.Square,
                                                        accum_out=c(1)), wait=[t_x], inc=self.se)
        t0 = em.emit("vector", lambda v: v.tensor_scalar(c(2), c(0), 1.0 / D, None, op0=ALU.mult),
                     wait=[t_sum], inc=self.ve)
        t1 = em.emit("vector", lambda v: v.tensor_mul(c(3), c(2), c(2)), wait=[t0], inc=self.ve)
        t2 = em.emit("vector", lambda v: v.tensor_scalar(c(4), c(1), 1.0 / D, EPS_LN,
                                                         op0=ALU.mult, op1=ALU.add),
                     wait=[t_sq, t1], inc=self.ve)
        t3 = em.emit("vector", lambda v: v.tensor_sub(c(4), c(4), c(3)), wait=[t2], inc=self.ve)
        t4 = em.emit("scalar", lambda s: s.activation(c(5), c(4), AF.Sqrt), wait=[t3], inc=self.se)
        t5 = em.emit("vector", lambda v: v.reciprocal(c(6), c(5)), wait=[t4], inc=self.ve)
        if not rms:
            return em.emit("vector", lambda v: v.tensor_scalar(out_slot[:, :], x_slot[:, :],
                                                               c(2), c(6), op0=ALU.add, op1=ALU.mult),
                           wait=[t5], inc=self.ve)
        t7 = em.emit("vector", lambda v: v.tensor_scalar(ln_f32_out[:, :], x_slot[:, :],
                                                         c(2), c(6), op0=ALU.add, op1=ALU.mult),
                     wait=[t5], inc=self.ve)
        t8 = em.emit("scalar", lambda s: s.activation(out_slot[:, :], ln_f32_out[:, :], AF.Square,
                                                      accum_out=c(1)), wait=[t7], inc=self.se)
        t9 = em.emit("vector", lambda v: v.tensor_scalar(c(4), c(1), 1.0 / D, EPS_LN,
                                                         op0=ALU.mult, op1=ALU.add),
                     wait=[t8], inc=self.ve)
        ta = em.emit("scalar", lambda s: s.activation(c(5), c(4), AF.Sqrt), wait=[t9], inc=self.se)
        tb = em.emit("vector", lambda v: v.reciprocal(c(6), c(5)), wait=[ta], inc=self.ve)
        return em.emit("scalar", lambda s: s.activation(out_slot[:, :], ln_f32_out[:, :], AF.Copy,
                                                        scale=c(6)), wait=[tb], inc=self.se), t7

    def transpose_tile(self, src_slot, mt, dest, tok_src):
        em = self.em
        last = None
        for cch in range(16):
            pp = cch % 2
            g = self._tt_guard[pp]
            tpe = em.emit("tensor",
                          (lambda cch, pp: lambda t: t.transpose(
                              [self.PBb, self.PDB][pp][:, 0:128],
                              src_slot[:, cch * 128:(cch + 1) * 128], self.IDT[:, :]))(cch, pp),
                          wait=[tok_src, self.static_tok] + ([g] if g else []), inc=self.pe)
            tcp = em.emit("vector",
                          (lambda cch, pp, mt: lambda v: v.tensor_copy(
                              dest[:, cch * 2048 + mt * 128:cch * 2048 + mt * 128 + 128],
                              [self.PBb, self.PDB][pp][:, 0:128]))(cch, pp, mt),
                          wait=[tpe], inc=self.ve)
            self._tt_guard[pp] = tcp
            last = tcp
        return last

    def dma_weight_unit(self, wslot_idx, dram_ap, cols0, width, wait=(), ffn=False):
        em = self.em
        slot = ([self.FWA, self.FWB] if ffn else [self.WA, self.WB])[wslot_idx]
        sem = self.dW[wslot_idx]
        src = dram_ap[:, cols0:cols0 + width].rearrange("(k p) f -> p k f", p=128)
        dst = slot[:, 0:16 * width].rearrange("p (k f) -> p k f", f=width)
        tok = em.dma("gpsimd", dst, src, sem, wait=wait)
        return tok, slot

    def rms_scale_unit(self, slot, width, t_w, t_rm):
        em = self.em
        tt = None
        for k in range(16):
            tt = em.emit("vector",
                         (lambda k: lambda v: v.tensor_scalar(
                             slot[:, k * width:(k + 1) * width], slot[:, k * width:(k + 1) * width],
                             self.RMW[:, k:k + 1], None, op0=ALU.mult))(k),
                         wait=[t_w, t_rm] if k == 0 else [t_w], inc=self.ve)
        return tt

    def bias_row(self, roff, dram_vec, lo, width):
        g = self._brow_tok.get(roff)
        tok = self.em.dma("gpsimd", self.BROW[0:1, roff:roff + width],
                          dram_vec[lo:lo + width][None, :], self.dB[roff // 512],
                          wait=[g] if g else [])
        return tok


    def p0(self):
        em, d = self.em, self.d
        XN, QD = self.R1, self.R2
        TBL = self.CCR
        SLc = [self.CCR[:, 8192:10240], self.CCR[:, 10240:12288]]
        self._tt_guard = [None, None]
        self._wslot_tok = [None, None]
        self._rg = {}

        t_tbl = em.dma("sync", TBL[:, 0:8192], self.tbl0_t.ap().bitcast(BF16), self.dST)
        self._brow_tok = {}

        import os as _os
        _nt = int(_os.environ.get("MK_LN_TILES", MT))
        slot_tok = None
        t_tr = None
        for mt in range(_nt):
            sl = mt % 2
            ts = self.TSa
            w = [slot_tok] if slot_tok else []
            t_x = em.dma("sync", ts[:, :], d["x"][mt * 128:(mt + 1) * 128, :], self.dX[0], wait=w)
            t_xn = self.ln_tile(ts, t_x, 0, SLc[sl])
            slot_tok = t_xn
            t_tr = self.transpose_tile(SLc[sl], mt, XN, t_xn)
        xn_tok = t_tr
        import os
        if os.environ.get("MK_STOP") == "ln":
            return

        def proj_units(wdram, nunits, out_emit, boff, bias_vec, rms=False, t_rm=None):
            wtok = self._wslot_tok
            pp_tok = [None, None]
            for u in range(nunits):
                wsl = u % 2
                w = [wtok[wsl]] if wtok[wsl] else []
                t_w, slot = self.dma_weight_unit(wsl, wdram, u * 512, 512, wait=w)
                t_row = self.bias_row(boff, bias_vec, u * 512, 512)
                if rms:
                    t_w = self.rms_scale_unit(slot, 512, t_w, t_rm)
                t_last = None
                for m in range(4):
                    for n in range(4):
                        pp = (m * 4 + n) % 2
                        ps = [self.PA, self.PC][pp]
                        wl = [t_w, t_row, xn_tok, t_tbl, self.static_tok]
                        if pp_tok[pp]:
                            wl.append(pp_tok[pp])
                        t_pre = em.emit("tensor",
                                (lambda m, ps, boff: lambda t: t.matmul(
                                    ps[:, :],
                                    self.BROW[0:1, boff + m * 128:boff + (m + 1) * 128],
                                    self.ONESB[0:1, 0:512],
                                    start=True, stop=False))(m, ps, boff),
                                wait=wl, inc=self.pe)
                        self._brow_tok[boff] = t_pre
                        for k in range(16):
                            t_mm = em.emit("tensor",
                                           (lambda k, m, n, ps, slot: lambda t: t.matmul(
                                               ps[:, :],
                                               slot[:, k * 512 + m * 128:k * 512 + (m + 1) * 128],
                                               XN[:, k * 2048 + n * 512:k * 2048 + (n + 1) * 512],
                                               start=False, stop=(k == 15)))(k, m, n, ps, slot),
                                           inc=self.pe if k == 15 else None)
                        pp_tok[pp] = out_emit(u * 4 + m, n, ps, t_mm)
                        t_last = t_mm
                wtok[wsl] = t_last
            return pp_tok


        def q_out(ch, n, ps, t_mm):
            sl_q = lambda: QD[:, ch * 2048 + n * 512:ch * 2048 + (n + 1) * 512]
            t_ev = em.emit("scalar",
                           (lambda ch, n, ps: lambda s: s.activation(
                               QD[:, ch * 2048 + n * 512:ch * 2048 + (n + 1) * 512],
                               ps[:, :], AF.Copy))(ch, n, ps),
                           wait=[t_mm], inc=self.se)
            return self.rope_l0(QD, ch, n, t_ev, 0, self._rg)

        proj_units(d["wq0"], 4, q_out, 0, d["bq0"])
        if os.environ.get("MK_STOP") == "q":
            return

        kdone = {}
        kslot_tok = [None, None]

        def k_out(ch, n, ps, t_mm):
            ksl = SLc[ch % 2]
            wl = [t_mm]
            if n == 0 and kslot_tok[ch % 2]:
                wl.append(kslot_tok[ch % 2])
            t_ev = em.emit("scalar",
                           (lambda ch, n, ps, ksl: lambda s: s.activation(
                               ksl[:, n * 512:(n + 1) * 512], ps[:, :], AF.Copy))(ch, n, ps, ksl),
                           wait=wl, inc=self.se)
            t_rope = self.rope_l0(ksl, 0, n, t_ev, 4096, self._rg)
            kdone.setdefault(ch, []).append(t_rope)
            if n == 3:
                toks = kdone[ch]
                t_dma = em.dma("sync", self.d["k_hbm"][ch * 128:(ch + 1) * 128, :],
                               ksl[:, :], self.dS[ch % 2], wait=toks)
                kslot_tok[ch % 2] = t_dma
            return t_rope

        proj_units(d["wk0"], 1, k_out, 512, d["bk0"])
        if os.environ.get("MK_STOP") == "k":
            return

        wv_wait = [self._wslot_tok[0]] if self._wslot_tok[0] else []
        t_wv, vslot = self.dma_weight_unit(0, d["wv0"], 0, 512, wait=wv_wait)
        t_bv = self.bias_row(1024, d["bv0"], 0, 512)
        vsl_tok = [kslot_tok[0], kslot_tok[1]]
        pp_tok = [None, None]
        for mt in range(MT):
            pp = mt % 2
            ps = [self.PA, self.PC][pp]
            wl = [t_wv, xn_tok, t_bv, self.static_tok]
            if pp_tok[pp]:
                wl.append(pp_tok[pp])
            t_pre = em.emit("tensor",
                            (lambda ps: lambda t: t.matmul(
                                ps[:, :], self.ONESB[0:1, 0:128], self.BROW[0:1, 1024:1536],
                                start=True, stop=False))(ps),
                            wait=wl, inc=self.pe)
            self._brow_tok[1024] = t_pre
            for k in range(16):
                t_mm = em.emit("tensor",
                               (lambda k, mt, ps: lambda t: t.matmul(
                                   ps[:, :], XN[:, k * 2048 + mt * 128:k * 2048 + mt * 128 + 128],
                                   vslot[:, k * 512:(k + 1) * 512],
                                   start=False, stop=(k == 15)))(k, mt, ps),
                               inc=self.pe if k == 15 else None)
            vsl = SLc[pp]
            wl2 = [t_mm]
            if vsl_tok[pp]:
                wl2.append(vsl_tok[pp])
            t_ev = em.emit("scalar",
                           (lambda vsl, ps: lambda s: s.activation(
                               vsl[:, 0:512], ps[:, :], AF.Copy))(vsl, ps),
                           wait=wl2, inc=self.se)
            t_dma = em.dma("sync", d["v_hbm"][mt * 128:(mt + 1) * 128, 0:512],
                           vsl[:, 0:512], self.dS[pp], wait=[t_ev])
            vsl_tok[pp] = t_dma
            pp_tok[pp] = t_ev

    def rope_l0(self, dst, ch, n, t_ev, tbl_off, guard):
        em = self.em
        TBL = self.CCR
        sp = lambda: dst[:, ch * 2048 + n * 512:ch * 2048 + (n + 1) * 512]
        TMP = self.TMPF[:, (n % 2) * 512:(n % 2) * 512 + 512]
        g = guard.get(("tmp", n % 2))
        t_sw = em.emit("tensor",
                       (lambda ch, n: lambda t: t.matmul(
                           self.PS[:, 0:512], self.PERM[:, :],
                           dst[:, ch * 2048 + n * 512:ch * 2048 + (n + 1) * 512],
                           start=True, stop=True))(ch, n),
                       wait=[t_ev, self.static_tok] + ([guard["pd"]] if "pd" in guard else []), inc=self.pe)
        t1 = em.emit("vector",
                     (lambda n, TMP: lambda v: v.tensor_mul(
                         TMP[:, :], self.PS[:, 0:512],
                         TBL[:, tbl_off + 2048 + n * 512:tbl_off + 2048 + (n + 1) * 512]))(n, TMP),
                     wait=[t_sw] + ([g] if g else []), inc=self.ve)
        guard["pd"] = t1
        t2 = em.emit("vector",
                     (lambda ch, n: lambda v: v.tensor_mul(
                         dst[:, ch * 2048 + n * 512:ch * 2048 + (n + 1) * 512],
                         dst[:, ch * 2048 + n * 512:ch * 2048 + (n + 1) * 512],
                         TBL[:, tbl_off + n * 512:tbl_off + (n + 1) * 512]))(ch, n),
                     wait=[t_sw], inc=self.ve)
        t3 = em.emit("vector",
                     (lambda ch, n, TMP: lambda v: v.tensor_add(
                         dst[:, ch * 2048 + n * 512:ch * 2048 + (n + 1) * 512],
                         dst[:, ch * 2048 + n * 512:ch * 2048 + (n + 1) * 512],
                         TMP[:, :]))(ch, n, TMP),
                     wait=[t1, t2], inc=self.ve)
        guard[("tmp", n % 2)] = t3
        return t3


    def p1(self):
        em, d = self.em, self.d
        QD, R1 = self.R2, self.R1
        CCR = self.CCR
        for s_ in range(8):
            em.dma("sync", R1[0:64, s_ * 2048:(s_ + 1) * 2048],
                   d["k_hbm"][s_ * 64:(s_ + 1) * 64, :], self.dR1)
            em.dma("sync", R1[64:128, s_ * 2048:(s_ + 1) * 2048],
                   d["k_hbm"][s_ * 64:(s_ + 1) * 64, :], self.dR1)
        t_k = (self.dR1, self.dR1.n)
        t_v = em.dma("sync", R1[:, 16384:24576].rearrange("p (j f) -> p j f", f=512),
                     d["v_hbm"][:, 0:512].rearrange("(j p) f -> p j f", p=128), self.dR1)
        self.attention(layer=0, t_kv=[t_k, t_v], QD=QD, KD=R1, R1=R1)

    def attention(self, layer, t_kv, QD, KD, R1):
        em, d = self.em, self.d
        CCR = self.CCR
        nheads = 32 if layer == 0 else 16
        dh = 64 if layer == 0 else 128
        ST = self.ST
        ysl_tok = [None, None]
        self._tt_guard = [None, None]
        ps_guard = []
        self._attg = [None, None]
        scf_guard = [None, None]
        pc_guard = None
        cp_flip = [0]
        pend = None
        for i in range(MT):
            W = (i + 1) * 128
            ysl = CCR[:, 8192 + (i % 2) * 2048:8192 + (i % 2) * 2048 + 2048]
            for h in range(nheads):
                hp = h % 2
                ATTs = CCR[:, hp * 2048:hp * 2048 + 2048]
                ATTTs = CCR[:, 4096 + hp * 2048:4096 + hp * 2048 + 2048]
                SCFs = self.SCF[hp]
                c0 = ST[:, 4 * hp:4 * hp + 1]
                c1 = ST[:, 4 * hp + 1:4 * hp + 2]
                c2 = ST[:, 4 * hp + 2:4 * hp + 3]
                if layer == 0:
                    qc, qr = h // 2, 64 * (h % 2)
                    kvh = h // 4
                    kc, kr = kvh, qr
                    vcol = (lambda kvh: lambda j: 16384 + j * 512 + kvh * 64)(kvh)
                else:
                    qc, qr = h // 2, 64 * (h % 2)
                    kc, kr = 8 + h // 2, 64 * (h % 2)
                    vcol = (lambda h: lambda j: j * 2048 + h * 128)(h)
                nspan = (W + 511) // 512
                wl0 = list(t_kv) + list(ps_guard)
                for s_ in range(nspan):
                    w0, w1 = s_ * 512, min(W, s_ * 512 + 512)
                    t_sc = em.emit("tensor",
                                   (lambda i, qc, qr, kc, kr, w0, w1: lambda t: t.matmul(
                                       self.PS[:, w0:w1],
                                       QD[qr:qr + 64, qc * 2048 + i * 128:qc * 2048 + (i + 1) * 128],
                                       KD[kr:kr + 64, kc * 2048 + w0:kc * 2048 + w1],
                                       start=True, stop=True))(i, qc, qr, kc, kr, w0, w1),
                                   wait=wl0 if s_ == 0 else (),
                                   inc=self.pe if s_ == nspan - 1 else None)
                if pend is not None:
                    pc_guard = self._attn_tail(**pend)
                t_mk = em.emit("vector",
                               (lambda i: lambda v: v.tensor_add(
                                   self.PS[:, i * 128:(i + 1) * 128],
                                   self.PS[:, i * 128:(i + 1) * 128], self.MASK[:, :]))(i),
                               wait=[t_sc, self.static_tok], inc=self.ve)
                t_mx = em.emit("vector",
                               (lambda W, c0: lambda v: v.reduce_max(
                                   c0, self.PS[:, 0:W], axis=AX.X, negate=True))(W, c0),
                               wait=[t_mk] + ([scf_guard[hp]] if scf_guard[hp] else []), inc=self.ve)
                wl_ev = [t_mk, t_mx]
                if scf_guard[hp]:
                    wl_ev.append(scf_guard[hp])
                t_ev = em.emit("scalar",
                               (lambda W, SCFs: lambda s: s.activation(
                                   SCFs[:, 0:W], self.PS[:, 0:W], AF.Copy))(W, SCFs),
                               wait=wl_ev, inc=self.se)
                ps_guard = [t_ev]
                wl_ex = [t_mx, t_ev]
                if self._attg[hp]:
                    wl_ex.append(self._attg[hp])
                t_ex = em.emit("scalar",
                               (lambda W, ATTs, SCFs, c0: lambda s: s.activation(
                                   ATTs[:, 0:W], SCFs[:, 0:W], AF.Exp, bias=c0))(W, ATTs, SCFs, c0),
                               wait=wl_ex, inc=self.se)
                scf_guard[hp] = t_ex
                t_sm = em.emit("vector",
                               (lambda W, ATTs, c1: lambda v: v.reduce_sum(
                                   c1, ATTs[:, 0:W], axis=AX.X))(W, ATTs, c1),
                               wait=[t_ex], inc=self.ve)
                t_rc = em.emit("vector", (lambda c1, c2: lambda v: v.reciprocal(c2, c1))(c1, c2),
                               wait=[t_sm], inc=self.ve)
                pend = dict(i=i, h=h, hp=hp, ATTs=ATTs, ATTTs=ATTTs, t_ex=t_ex, t_rc=t_rc,
                            vcol=vcol, dh=dh, c2=c2, ysl=ysl, R1=R1,
                            y_wait=(ysl_tok[i % 2] if h == 0 else None), cp_flip=cp_flip,
                            pc_guard_in=pc_guard)
                if h == nheads - 1:
                    pc_guard = self._attn_tail(**pend)
                    pend = None
                    if True:
                        pass
            t_y = em.dma("sync", d["y_hbm"][i * 128:(i + 1) * 128, :], ysl[:, :],
                         self.dS[i % 2], wait=[self._last_yevac])
            ysl_tok[i % 2] = t_y

    def _attn_tail(self, i, h, hp, ATTs, ATTTs, t_ex, t_rc, vcol, dh, c2, ysl, R1,
                   y_wait, cp_flip, pc_guard_in):
        em = self.em
        t_cp = None
        cp_last = {}
        att_done = None
        for j in range(i + 1):
            pp = j % 2
            g = self._tt_guard[pp]
            t_tp = em.emit("tensor",
                           (lambda j, pp, ATTs: lambda t: t.transpose(
                               [self.PBb, self.PDB][pp][:, 0:128],
                               ATTs[:, j * 128:(j + 1) * 128], self.IDT[:, :]))(j, pp, ATTs),
                           wait=[t_ex, self.static_tok] + ([g] if g else []), inc=self.pe)
            att_done = t_tp
            eng = "vector" if (cp_flip[0] % 2 == 0) else "scalar"
            cp_flip[0] += 1
            t_cp = em.emit(eng,
                           (lambda j, pp, ATTTs: lambda e: e.tensor_copy(
                               ATTTs[:, j * 128:(j + 1) * 128],
                               [self.PBb, self.PDB][pp][:, 0:128]))(j, pp, ATTTs)
                           if eng == "vector" else
                           (lambda j, pp, ATTTs: lambda e: e.copy(
                               ATTTs[:, j * 128:(j + 1) * 128],
                               [self.PBb, self.PDB][pp][:, 0:128]))(j, pp, ATTTs),
                           wait=[t_tp], inc=self.ve if eng == "vector" else self.se)
            self._tt_guard[pp] = t_cp
            cp_last[eng] = t_cp
        self._attg[hp] = att_done
        wl = list(cp_last.values()) + ([pc_guard_in] if pc_guard_in else [])
        for j in range(i + 1):
            t_av = em.emit("tensor",
                           (lambda j, i, ATTTs, vcol, dh: lambda t: t.matmul(
                               self.PC[:, 0:dh],
                               ATTTs[:, j * 128:(j + 1) * 128],
                               R1[:, vcol(j):vcol(j) + dh],
                               start=(j == 0), stop=(j == i)))(j, i, ATTTs, vcol, dh),
                           wait=wl if j == 0 else (),
                           inc=self.pe if j == i else None)
        wl2 = [t_av, t_rc]
        if y_wait:
            wl2.append(y_wait)
        t_evy = em.emit("scalar",
                        (lambda h, ysl, dh, c2: lambda s: s.activation(
                            ysl[:, h * dh:(h + 1) * dh], self.PC[:, 0:dh],
                            AF.Copy, scale=c2))(h, ysl, dh, c2),
                        wait=wl2, inc=self.se)
        self._last_yevac = t_evy
        return t_evy


    def p_out(self, layer):
        em, d = self.em, self.d
        R1, R2 = self.R1, self.R2
        CCR = self.CCR
        self._tt_guard = [None, None]
        wo = d["wo0"] if layer == 0 else d["wo1"]
        bo_vec = d["bo0"] if layer == 0 else d["bo1"]
        self._brow_tok = {}
        t_wo = em.dma("gpsimd", R2[:, :].rearrange("p (k f) -> p k f", f=2048),
                      wo.rearrange("(k p) f -> p k f", p=128), self.dW[2])
        ysl_tok = [None, None]
        t_tr = None
        for mt in range(MT):
            sl = mt % 2
            ysl = CCR[:, sl * 2048:sl * 2048 + 2048]
            wly = [ysl_tok[sl]] if ysl_tok[sl] else []
            t_yl = em.dma("sync", ysl[:, :], d["y_hbm"][mt * 128:(mt + 1) * 128, :],
                          self.dS[sl], wait=wly)
            t_tr = self.transpose_tile(ysl, mt, R1, t_yl)
            ysl_tok[sl] = t_tr
        yfm_tok = t_tr
        x_src = d["x"] if layer == 0 else d["r_hbm"]
        x_dst = d["xA"] if layer == 0 else d["xC"]
        pp_tok = [None, None]
        for mt in range(MT):
            sl = mt % 2
            ts = [self.TSa, self.TSb][sl]
            t_x0 = em.dma("sync", ts[:, :], x_src[mt * 128:(mt + 1) * 128, :], self.dX[sl],
                          wait=[self.dOUT] if mt >= 2 else [])
            adds = []
            for u in range(4):
                pp = u % 2
                ps = [self.PA, self.PC][pp]
                t_bo = self.bias_row(1024, bo_vec, u * 512, 512)
                wl = [yfm_tok, t_wo, t_bo, self.static_tok]
                if pp_tok[pp]:
                    wl.append(pp_tok[pp])
                t_pre = em.emit("tensor",
                        (lambda u, ps: lambda t: t.matmul(
                            ps[:, :], self.ONESB[0:1, 0:128],
                            self.BROW[0:1, 1024:1536],
                            start=True, stop=False))(u, ps),
                        wait=wl, inc=self.pe)
                self._brow_tok[1024] = t_pre
                for k in range(16):
                    t_mm = em.emit("tensor",
                                   (lambda k, u, mt, ps: lambda t: t.matmul(
                                       ps[:, :], R1[:, k * 2048 + mt * 128:k * 2048 + mt * 128 + 128],
                                       R2[:, k * 2048 + u * 512:k * 2048 + (u + 1) * 512],
                                       start=False, stop=(k == 15)))(k, u, mt, ps),
                                   inc=self.pe if k == 15 else None)
                t_ad = em.emit("vector",
                               (lambda u, ts, ps: lambda v: v.tensor_add(
                                   ts[:, u * 512:(u + 1) * 512], ps[:, :],
                                   ts[:, u * 512:(u + 1) * 512]))(u, ts, ps),
                               wait=[t_mm, t_x0], inc=self.ve)
                pp_tok[pp] = t_ad
                adds.append(t_ad)
            t_st = em.dma("sync", x_dst[mt * 128:(mt + 1) * 128, :], ts[:, :], self.dOUT, wait=adds)
        self.p_out_b(layer)

    def p_out_b(self, layer):
        em, d = self.em, self.d
        R1 = self.R1
        CCR = self.CCR
        self._tt_guard = [None, None]
        x_src = d["xA"] if layer == 0 else d["xC"]
        xn_dst = d["xnA"] if layer == 0 else d["xnC"]
        cmb_dst = d["cmbA"] if layer == 0 else d["cmbC"]
        ne = 8 if layer == 0 else 16
        stride = 10 if layer == 0 else 17
        gw = d["gw0"] if layer == 0 else d["gw1"]
        wga = [self._wslot_tok[0]] if self._wslot_tok[0] else []
        t_gw = em.dma("gpsimd", self.WA[:, 0:16 * ne].rearrange("p (k f) -> p k f", f=ne),
                      gw.rearrange("(k p) f -> p k f", p=128), self.dW[0], wait=wga)
        slot_tok = [None, None]
        self._route_guard = None
        for mt in range(MT):
            sl = mt % 2
            ts = [self.TSa, self.TSb][sl]
            xsl = CCR[:, sl * 2048:sl * 2048 + 2048]
            wx = [self.dOUT] if mt < 2 else []
            if slot_tok[sl]:
                wx.append(slot_tok[sl])
            t_x1 = em.dma("sync", ts[:, :], x_src[mt * 128:(mt + 1) * 128, :], self.dX[sl], wait=wx)
            t_xn = self.ln_tile(ts, t_x1, 8 * sl, xsl)
            slot_tok[sl] = t_xn
            t_tr = self.transpose_tile(xsl, mt, R1, t_xn)
            for k in range(16):
                t_lg = em.emit("tensor",
                               (lambda k, mt, ne: lambda t: t.matmul(
                                   self.PC[:, 0:ne], R1[:, k * 2048 + mt * 128:k * 2048 + mt * 128 + 128],
                                   self.WA[:, k * ne:(k + 1) * ne],
                                   start=(k == 0), stop=(k == 15)))(k, mt, ne),
                               wait=([t_tr, t_gw] + ([self._route_guard] if self._route_guard else [])) if k == 0 else (),
                               inc=self.pe if k == 15 else None)
            self._route_guard = self.routing(layer, mt, t_lg, cmb_dst, stride, ne)
        last = None
        for kk in range(16):
            last = em.dma("sync", xn_dst[kk * 128:(kk + 1) * 128, :],
                          R1[:, kk * 2048:(kk + 1) * 2048], self.dOUT,
                          wait=[self._tt_guard[0], self._tt_guard[1], self._route_guard])
        em.dma("sync", cmb_dst, self.CMB[:, 0:MT * stride], self.dOUT, wait=[self._route_guard])

    def routing(self, layer, mt, t_lg, cmb_dst, stride, ne):
        em = self.em
        ST = self.ST
        SC = ST[:, 24:24 + ne]
        SEL = ST[:, 48:48 + ne]
        T2 = ST[:, 64:64 + ne]
        c = lambda i: ST[:, 16 + i:16 + i + 1]
        CMB = self.CMB
        if layer == 0:
            t0 = em.emit("vector", lambda v: v.reduce_max(c(0), self.PC[:, 0:ne], axis=AX.X, negate=True),
                         wait=[t_lg], inc=self.ve)
            t1 = em.emit("scalar", (lambda ne: lambda s: s.activation(SC[:, :], self.PC[:, 0:ne],
                                                                      AF.Exp, bias=c(0)))(ne),
                         wait=[t0], inc=self.se)
            t2 = em.emit("vector", lambda v: v.reduce_sum(c(1), SC[:, :], axis=AX.X), wait=[t1], inc=self.ve)
            t3 = em.emit("vector", lambda v: v.reciprocal(c(2), c(1)), wait=[t2], inc=self.ve)
            t4 = em.emit("vector", lambda v: v.tensor_scalar(SC[:, :], SC[:, :], c(2), None, op0=ALU.mult),
                         wait=[t3], inc=self.ve)
            tsc = t4
        else:
            tsc = em.emit("scalar", (lambda ne: lambda s: s.activation(SC[:, :], self.PC[:, 0:ne],
                                                                       AF.Sigmoid))(ne),
                          wait=[t_lg], inc=self.se)
            GS = ST[:, 40:44]
            gs_toks = []
            prev_te = None
            for g in range(4):
                Sg = SC[:, g * 4:(g + 1) * 4]
                ta = em.emit("vector", (lambda Sg: lambda v: v.reduce_max(c(0), Sg, axis=AX.X))(Sg),
                             wait=[tsc] + ([prev_te] if prev_te else []), inc=self.ve)
                tb = em.emit("vector", (lambda Sg, g: lambda v: v.tensor_scalar(
                    SEL[:, g * 4:(g + 1) * 4], Sg, c(0), None, op0=ALU.is_ge))(Sg, g),
                    wait=[ta], inc=self.ve)
                tc = em.emit("vector", (lambda Sg, g: lambda v: v.scalar_tensor_tensor(
                    T2[:, g * 4:(g + 1) * 4], SEL[:, g * 4:(g + 1) * 4], NEG, Sg,
                    op0=ALU.mult, op1=ALU.add))(Sg, g),
                    wait=[tb], inc=self.ve)
                td = em.emit("vector", (lambda g: lambda v: v.reduce_max(
                    c(1), T2[:, g * 4:(g + 1) * 4], axis=AX.X))(g),
                    wait=[tc], inc=self.ve)
                te = em.emit("vector", (lambda g: lambda v: v.tensor_add(GS[:, g:g + 1], c(0), c(1)))(g),
                             wait=[td], inc=self.ve)
                gs_toks.append(te)
                prev_te = te
            ta = em.emit("vector", lambda v: v.reduce_max(c(0), GS[:, :], axis=AX.X), wait=gs_toks, inc=self.ve)
            tb = em.emit("vector", lambda v: v.tensor_scalar(T2[:, 0:4], GS[:, :], c(0), None, op0=ALU.is_ge),
                         wait=[ta], inc=self.ve)
            tc = em.emit("vector", lambda v: v.scalar_tensor_tensor(T2[:, 4:8], T2[:, 0:4], NEG, GS[:, :],
                                                                   op0=ALU.mult, op1=ALU.add),
                         wait=[tb], inc=self.ve)
            td = em.emit("vector", lambda v: v.reduce_max(c(1), T2[:, 4:8], axis=AX.X), wait=[tc], inc=self.ve)
            tgs = em.emit("vector", lambda v: v.tensor_scalar(T2[:, 0:4], GS[:, :], c(1), None, op0=ALU.is_ge),
                          wait=[td], inc=self.ve)
            msk_toks = []
            for g in range(4):
                tm = em.emit("vector", (lambda g: lambda v: v.tensor_scalar(
                    SC[:, g * 4:(g + 1) * 4], SC[:, g * 4:(g + 1) * 4], T2[:, g:g + 1], None,
                    op0=ALU.mult))(g),
                    wait=[tgs], inc=self.ve)
                msk_toks.append(tm)
            tsc = msk_toks[-1]
            tsc = em.emit("vector", lambda v: v.tensor_copy(ST[:, 8:9], ST[:, 8:9]), wait=msk_toks, inc=self.ve)
        ta = em.emit("vector", (lambda ne: lambda v: v.reduce_max(c(3), SC[:, :], axis=AX.X))(ne),
                     wait=[tsc], inc=self.ve)
        tb = em.emit("vector", (lambda ne: lambda v: v.tensor_scalar(SEL[:, :], SC[:, :], c(3), None,
                                                                    op0=ALU.is_ge))(ne),
                     wait=[ta], inc=self.ve)
        tc = em.emit("vector", (lambda ne: lambda v: v.scalar_tensor_tensor(T2[:, 0:ne], SEL[:, :], NEG,
                                                                            SC[:, :], op0=ALU.mult,
                                                                            op1=ALU.add))(ne),
                     wait=[tb], inc=self.ve)
        td = em.emit("vector", (lambda ne: lambda v: v.reduce_max(c(4), T2[:, 0:ne], axis=AX.X))(ne),
                     wait=[tc], inc=self.ve)
        te = em.emit("vector", (lambda ne: lambda v: v.tensor_scalar(SEL[:, :], SC[:, :], c(4), None,
                                                                    op0=ALU.is_ge))(ne),
                     wait=[td], inc=self.ve)
        tw = em.emit("vector", (lambda ne: lambda v: v.tensor_mul(T2[:, 0:ne], SEL[:, :], SC[:, :]))(ne),
                     wait=[te], inc=self.ve)
        if layer == 0:
            out = em.emit("vector", (lambda mt, ne, stride: lambda v: v.tensor_copy(
                CMB[:, mt * stride:mt * stride + ne], T2[:, 0:ne]))(mt, ne, stride),
                wait=[tw], inc=self.ve)
        else:
            ts_ = em.emit("vector", (lambda ne: lambda v: v.reduce_sum(c(5), T2[:, 0:ne], axis=AX.X))(ne),
                          wait=[tw], inc=self.ve)
            ts2 = em.emit("vector", lambda v: v.tensor_scalar(c(5), c(5), 1e-20, None, op0=ALU.add),
                          wait=[ts_], inc=self.ve)
            ts3 = em.emit("vector", lambda v: v.reciprocal(c(6), c(5)), wait=[ts2], inc=self.ve)
            out = em.emit("vector", (lambda mt, ne, stride: lambda v: v.tensor_scalar(
                CMB[:, mt * stride:mt * stride + ne], T2[:, 0:ne], c(6), None, op0=ALU.mult))(mt, ne, stride),
                wait=[ts3], inc=self.ve)
        nones = stride - ne
        out = em.emit("vector", (lambda mt, ne, stride: lambda v: v.memset(
            CMB[:, mt * stride + ne:(mt + 1) * stride], 1.0))(mt, ne, stride),
            wait=[out], inc=self.ve)
        return out


    def p_ffn(self, layer):
        em, d = self.em, self.d
        R1, ACC, WC = self.R1, self.ACC, self.WC
        if layer == 0:
            units = [ (d["wg0"][e], d["wu0"][e], d["wd0"][e], e) for e in range(8) ]
            units.append((d["swg0"][:, 0:1408], d["swu0"][:, 0:1408], d["swd0"][0:1408, :], 8))
            units.append((d["swg0"][:, 1408:2816], d["swu0"][:, 1408:2816], d["swd0"][1408:2816, :], 9))
            stride = 10
            x_src, out_dst, xn_src, cmb_src = d["xA"], d["xB"], d["xnA"], d["cmbA"]
        else:
            units = [ (d["w1"][e], None, d["w2"][e], e) for e in range(16) ]
            units.append((d["sw1"], None, d["sw2"], 16))
            stride = 17
            x_src, out_dst, xn_src, cmb_src = d["xC"], d["out"], d["xnC"], d["cmbC"]
        import os
        if os.environ.get("MK_FFN_NE"):
            units = units[:int(os.environ["MK_FFN_NE"])]
        t_cmb = em.dma("sync", self.CMB[:, 0:MT * stride], cmb_src, self.dST)
        subw = [384, 384, 384, 256]
        sub0 = [0, 384, 768, 1152]
        subm = [3, 3, 3, 2]
        GOF = 16384
        USF = 27648
        wslot_tok = [None, None]
        wc_tok = None
        xnh_guard = None
        acc_store_toks = []
        ev_guard = [None, None]
        for half in (0, 1):
            accw = [self.dOUT] if acc_store_toks else []
            acc_toks = []
            for q in range(8):
                tk = em.dma("sync", ACC[:, q * 2048:(q + 1) * 2048],
                            x_src[(half * 8 + q) * 128:(half * 8 + q) * 128 + 128, :],
                            self.dR2, wait=accw if q == 0 else ())
                acc_toks.append(tk)
            t_acc = acc_toks[-1]
            xw = [xnh_guard] if xnh_guard else []
            for k in range(16):
                t_xn = em.dma("sync", R1[:, k * 1024:(k + 1) * 1024],
                              xn_src[k * 128:(k + 1) * 128, half * 1024:(half + 1) * 1024],
                              self.dR1, wait=xw if k == 0 else ())
            acc_store_toks = []
            last_umm = None
            for (wg_ap, wu_ap, wd_ap, ecol) in units:
                nslab = wd_ap.shape[0] // 128
                silu_toks = []
                g_toks = []
                prev_sub_gdone = None
                for sub in range(4):
                    w0, wid, nm = sub0[sub], subw[sub], subm[sub]
                    sub_silu = []
                    wsl = 0 if layer == 0 else (sub % 2)
                    wg_wait = [wslot_tok[wsl]] if wslot_tok[wsl] else []
                    t_wg, gslot = self.dma_weight_unit(wsl, wg_ap, w0, wid, wait=wg_wait, ffn=True)
                    pp_t = [None, None]
                    for m in range(nm):
                        for n in range(2):
                            pp = (m * 2 + n) % 2
                            ps = self.PS[:, pp * 512:(pp + 1) * 512]
                            wl = [t_wg, t_xn]
                            if m == 0 and n == 0 and prev_sub_gdone is not None:
                                wl.append(prev_sub_gdone)
                            if pp_t[pp]:
                                wl.append(pp_t[pp])
                            for k in range(16):
                                t_mm = em.emit("tensor",
                                               (lambda k, m, n, ps, gslot, wid: lambda t: t.matmul(
                                                   ps,
                                                   gslot[:, k * wid + m * 128:k * wid + (m + 1) * 128],
                                                   R1[:, k * 1024 + n * 512:k * 1024 + (n + 1) * 512],
                                                   start=(k == 0), stop=(k == 15)))(k, m, n, ps, gslot, wid),
                                               wait=wl if k == 0 else (),
                                               inc=self.pe if k == 15 else None)
                            last_umm = t_mm
                            if layer == 0:
                                dst = R1[:, USF + m * 1024 + n * 512:USF + m * 1024 + (n + 1) * 512]
                            else:
                                dst = R1[:, GOF + (sub0[sub] // 128 + m) * 1024 + n * 512:
                                         GOF + (sub0[sub] // 128 + m) * 1024 + (n + 1) * 512]
                            t_si = em.emit("scalar",
                                           (lambda ps, dst: lambda s: s.activation(dst, ps, AF.Silu))(ps, dst),
                                           wait=[t_mm], inc=self.se)
                            pp_t[pp] = t_si
                            silu_toks.append(t_si)
                            sub_silu.append(t_si)
                    wslot_tok[wsl] = last_umm
                    if layer == 0:
                        wsl2 = 1
                        wu_wait = [wslot_tok[wsl2]] if wslot_tok[wsl2] else []
                        t_wu, uslot = self.dma_weight_unit(wsl2, wu_ap, w0, wid, wait=wu_wait, ffn=True)
                        pp_t2 = [None, None]
                        for m in range(nm):
                            for n in range(2):
                                pp = (m * 2 + n) % 2
                                ps = self.PS[:, 1024 + pp * 512:1024 + (pp + 1) * 512]
                                wl = [t_wu, t_xn]
                                if pp_t2[pp]:
                                    wl.append(pp_t2[pp])
                                for k in range(16):
                                    t_mm = em.emit("tensor",
                                                   (lambda k, m, n, ps, uslot, wid: lambda t: t.matmul(
                                                       ps,
                                                       uslot[:, k * wid + m * 128:k * wid + (m + 1) * 128],
                                                       R1[:, k * 1024 + n * 512:k * 1024 + (n + 1) * 512],
                                                       start=(k == 0), stop=(k == 15)))(k, m, n, ps, uslot, wid),
                                                   wait=wl if k == 0 else (),
                                                   inc=self.pe if k == 15 else None)
                                last_umm = t_mm
                                gdst = R1[:, GOF + (sub0[sub] // 128 + m) * 1024 + n * 512:
                                          GOF + (sub0[sub] // 128 + m) * 1024 + (n + 1) * 512]
                                usrc = R1[:, USF + m * 1024 + n * 512:USF + m * 1024 + (n + 1) * 512]
                                t_g = em.emit("vector",
                                              (lambda gdst, usrc, ps: lambda v: v.tensor_mul(
                                                  gdst, usrc, ps))(gdst, usrc, ps),
                                              wait=[t_mm, sub_silu[m * 2 + n]],
                                              inc=self.ve)
                                g_toks.append(t_g)
                                pp_t2[pp] = t_g
                        wslot_tok[wsl2] = last_umm
                        prev_sub_gdone = g_toks[-1]
                    else:
                        g_toks.extend(sub_silu)
                        prev_sub_gdone = sub_silu[-1]
                wcw = [wc_tok] if wc_tok else []
                t_wc = em.dma("gpsimd", WC[:, 0:nslab * 2048].rearrange("p (k f) -> p k f", f=2048),
                              wd_ap.rearrange("(k p) f -> p k f", p=128), self.dW[2], wait=wcw)
                g_done = g_toks[-1]
                for mtq in range(8):
                    for nn in range(4):
                        pp = (mtq * 4 + nn) % 2
                        psd = [self.PA, self.PC][pp]
                        wl = [t_wc, g_done]
                        if ev_guard[pp]:
                            wl.append(ev_guard[pp])
                        for k in range(nslab):
                            t_dm = em.emit("tensor",
                                           (lambda k, mtq, nn, psd, nslab: lambda t: t.matmul(
                                               psd[:, :],
                                               R1[:, GOF + k * 1024 + mtq * 128:GOF + k * 1024 + mtq * 128 + 128],
                                               WC[:, k * 2048 + nn * 512:k * 2048 + (nn + 1) * 512],
                                               start=(k == 0), stop=(k == nslab - 1)))(k, mtq, nn, psd, nslab),
                                           wait=wl if k == 0 else (),
                                           inc=self.pe if k == nslab - 1 else None)
                        tile_idx = half * 8 + mtq
                        t_ev = em.emit("vector",
                                       (lambda mtq, nn, psd, tile_idx, ecol, stride: lambda v: v.scalar_tensor_tensor(
                                           ACC[:, mtq * 2048 + nn * 512:mtq * 2048 + (nn + 1) * 512],
                                           psd[:, :],
                                           self.CMB[:, tile_idx * stride + ecol:tile_idx * stride + ecol + 1],
                                           ACC[:, mtq * 2048 + nn * 512:mtq * 2048 + (nn + 1) * 512],
                                           op0=ALU.mult, op1=ALU.add))(mtq, nn, psd, tile_idx, ecol, stride),
                                       wait=[t_dm, t_acc, t_cmb], inc=self.ve)
                        ev_guard[pp] = t_ev
                wc_tok = t_dm
            xnh_guard = last_umm
            for q in range(8):
                tk = em.dma("sync", out_dst[(half * 8 + q) * 128:(half * 8 + q) * 128 + 128, :],
                            ACC[:, q * 2048:(q + 1) * 2048], self.dOUT,
                            wait=[ev_guard[0], ev_guard[1]] if q == 0 else ())
                acc_store_toks.append(tk)


    def p4(self):
        em, d = self.em, self.d
        XN, R2 = self.R1, self.R2
        TBL = self.CCR
        SLc = [self.CCR[:, 8192:10240], self.CCR[:, 10240:12288]]
        LNF = self.TSb[:, :]
        self._tt_guard = [None, None]
        self._wslot_tok = [None, None]

        t_tbl = em.dma("sync", TBL[:, 0:8192], self.tbl1_t.ap().bitcast(BF16), self.dST)
        t_rm = em.dma("sync", self.RMW[:, :], d["rmsw"], self.dST)
        self._brow_tok = {}

        t_tr = None
        for mt in range(MT):
            sl = mt % 2
            ts = self.TSa
            w = [self.dOUT] if mt >= 1 else []
            t_x = em.dma("sync", ts[:, :], d["xB"][mt * 128:(mt + 1) * 128, :], self.dX[0], wait=w)
            t_xn, t_ln = self.ln_tile(ts, t_x, 8 * sl, SLc[sl], rms=True, ln_f32_out=LNF)
            t_r = em.emit("vector", (lambda ts: lambda v: v.tensor_add(ts[:, :], ts[:, :], LNF[:, :]))(ts),
                          wait=[t_xn], inc=self.ve)
            t_rs = em.dma("sync", d["r_hbm"][mt * 128:(mt + 1) * 128, :], ts[:, :], self.dOUT, wait=[t_r])
            t_tr = self.transpose_tile(SLc[sl], mt, XN, t_xn)
        xn_tok = t_tr

        def proj_h(wdram, hq_base, tbl_q, boff, bias_vec):
            qsl_tok = [None, None]
            pp_tok = [None, None]
            hb_state = {}
            for u in range(4):
                wsl = u % 2
                w = [self._wslot_tok[wsl]] if self._wslot_tok[wsl] else []
                t_w, slot = self.dma_weight_unit(wsl, wdram, u * 512, 512, wait=w)
                t_row = self.bias_row(boff, bias_vec, u * 512, 512)
                t_w = self.rms_scale_unit(slot, 512, t_w, t_rm)
                t_last = None
                for m in range(4):
                    ch = u * 4 + m
                    qsl = SLc[ch % 2]
                    ch_evs = []
                    for n in range(4):
                        pp = (m * 4 + n) % 2
                        ps = [self.PA, self.PC][pp]
                        wl = [t_w, t_row, xn_tok, t_tbl, self.static_tok]
                        if pp_tok[pp]:
                            wl.append(pp_tok[pp])
                        if n == 0 and qsl_tok[ch % 2]:
                            wl.append(qsl_tok[ch % 2])
                        t_pre = em.emit("tensor",
                                (lambda m, ps, boff: lambda t: t.matmul(
                                    ps[:, :],
                                    self.BROW[0:1, boff + m * 128:boff + (m + 1) * 128],
                                    self.ONESB[0:1, 0:512],
                                    start=True, stop=False))(m, ps, boff),
                                wait=wl, inc=self.pe)
                        self._brow_tok[boff] = t_pre
                        for k in range(16):
                            t_mm = em.emit("tensor",
                                           (lambda k, m, n, ps, slot: lambda t: t.matmul(
                                               ps[:, :],
                                               slot[:, k * 512 + m * 128:k * 512 + (m + 1) * 128],
                                               XN[:, k * 2048 + n * 512:k * 2048 + (n + 1) * 512],
                                               start=False, stop=(k == 15)))(k, m, n, ps, slot),
                                           inc=self.pe if k == 15 else None)
                        t_ev = em.emit("scalar",
                                       (lambda ch, n, ps, qsl: lambda s: s.activation(
                                           qsl[:, n * 512:(n + 1) * 512], ps[:, :], AF.Copy))(ch, n, ps, qsl),
                                       wait=[t_mm], inc=self.se)
                        pp_tok[pp] = t_ev
                        ch_evs.append(t_ev)
                        t_last = t_mm
                    t_hb = self.hbuild(qsl, ch, hq_base, tbl_q, ch_evs, hb_state)
                    qsl_tok[ch % 2] = t_hb
                self._wslot_tok[wsl] = t_last
        proj_h(d["wq1"], 0, 0, 0, d["bq1"])
        proj_h(d["wk1"], 16384, 4096, 512, d["bk1"])

        vsl_tok = [None, None]
        pp_tok = [None, None]
        for u in range(4):
            wsl = u % 2
            w = [self._wslot_tok[wsl]] if self._wslot_tok[wsl] else []
            t_w, vslot = self.dma_weight_unit(wsl, d["wv1"], u * 512, 512, wait=w)
            t_bv = self.bias_row(1024, d["bv1"], u * 512, 512)
            t_w = self.rms_scale_unit(vslot, 512, t_w, t_rm)
            t_last = None
            for mt in range(MT):
                pp = mt % 2
                ps = [self.PA, self.PC][pp]
                wl = [t_w, xn_tok, t_bv, self.static_tok]
                if pp_tok[pp]:
                    wl.append(pp_tok[pp])
                t_pre = em.emit("tensor",
                        (lambda u, ps: lambda t: t.matmul(
                            ps[:, :], self.ONESB[0:1, 0:128],
                            self.BROW[0:1, 1024:1536],
                            start=True, stop=False))(u, ps),
                        wait=wl, inc=self.pe)
                self._brow_tok[1024] = t_pre
                for k in range(16):
                    t_mm = em.emit("tensor",
                                   (lambda k, mt, ps, vslot: lambda t: t.matmul(
                                       ps[:, :], XN[:, k * 2048 + mt * 128:k * 2048 + mt * 128 + 128],
                                       vslot[:, k * 512:(k + 1) * 512],
                                       start=False, stop=(k == 15)))(k, mt, ps, vslot),
                                   inc=self.pe if k == 15 else None)
                vsl = SLc[pp]
                wl2 = [t_mm]
                if vsl_tok[pp]:
                    wl2.append(vsl_tok[pp])
                t_ev = em.emit("scalar",
                               (lambda vsl, ps: lambda s: s.activation(
                                   vsl[:, 0:512], ps[:, :], AF.Copy))(vsl, ps),
                               wait=wl2, inc=self.se)
                t_dma = em.dma("sync", d["v_hbm"][mt * 128:(mt + 1) * 128, u * 512:(u + 1) * 512],
                               vsl[:, 0:512], self.dS[pp], wait=[t_ev])
                vsl_tok[pp] = t_dma
                pp_tok[pp] = t_ev
                t_last = t_mm
            self._wslot_tok[wsl] = t_last

    def hbuild(self, qsl, ch, hq_base, tbl_q, ch_evs, state):
        em = self.em
        TBL = self.CCR
        R2 = self.R2
        rr = 64 * (ch % 2)
        hc = ch // 2
        to_off = tbl_q if ch % 2 == 0 else tbl_q + 2048
        tw_off = tbl_q + 2048 if ch % 2 == 0 else tbl_q
        last = None
        for n in range(4):
            TMP = self.TMPF[:, (n % 2) * 512:(n % 2) * 512 + 512]
            g = state.get(("tmp", n % 2))
            t_sw = em.emit("tensor",
                           (lambda n, qsl: lambda t: t.matmul(
                               self.PS[:, 0:512], self.PERM1[:, :],
                               qsl[:, n * 512:(n + 1) * 512],
                               start=True, stop=True))(n, qsl),
                           wait=[ch_evs[n], self.static_tok] + ([state["pd"]] if "pd" in state else []), inc=self.pe)
            t1 = em.emit("vector",
                         (lambda n, TMP, rr, tw_off: lambda v: v.tensor_mul(
                             TMP[rr:rr + 64, :], self.PS[rr:rr + 64, 0:512],
                             TBL[rr:rr + 64, tw_off + n * 512:tw_off + (n + 1) * 512]))(n, TMP, rr, tw_off),
                         wait=[t_sw] + ([g] if g else []), inc=self.ve)
            state["pd"] = t1
            t2 = em.emit("vector",
                         (lambda n, qsl, rr, hc, to_off: lambda v: v.tensor_mul(
                             R2[rr:rr + 64, hq_base + hc * 2048 + n * 512:hq_base + hc * 2048 + (n + 1) * 512],
                             qsl[rr:rr + 64, n * 512:(n + 1) * 512],
                             TBL[rr:rr + 64, to_off + n * 512:to_off + (n + 1) * 512]))(n, qsl, rr, hc, to_off),
                         wait=[ch_evs[n]] + ([state.get(("h", hc, rr, n))] if state.get(("h", hc, rr, n)) else []),
                         inc=self.ve)
            t3 = em.emit("vector",
                         (lambda n, rr, hc, TMP: lambda v: v.tensor_add(
                             R2[rr:rr + 64, hq_base + hc * 2048 + n * 512:hq_base + hc * 2048 + (n + 1) * 512],
                             R2[rr:rr + 64, hq_base + hc * 2048 + n * 512:hq_base + hc * 2048 + (n + 1) * 512],
                             TMP[rr:rr + 64, :]))(n, rr, hc, TMP),
                         wait=[t1, t2], inc=self.ve)
            state[("tmp", n % 2)] = t3
            last = t3
        return last


    def p5(self):
        em, d = self.em, self.d
        t_v = em.dma("sync", self.R1[:, :].rearrange("p (j f) -> p j f", f=2048),
                     d["v_hbm"].rearrange("(j p) f -> p j f", p=128), self.dR1)
        self.attention(layer=1, t_kv=[t_v], QD=self.R2, KD=self.R2, R1=self.R1)


    def build(self):
        em = self.em
        em.dma("sync", self.IDT[:, :], self.ident_t.ap().bitcast(BF16), self.dST)
        em.dma("sync", self.ONESB[0:1, :], self.ones_t.ap().bitcast(BF16), self.dST)
        em.dma("sync", self.PERM[:, :], self.perm_t.ap().bitcast(BF16), self.dST)
        em.dma("sync", self.PERM1[:, :], self.perm1_t.ap().bitcast(BF16), self.dST)
        em.dma("sync", self.MASK[:, :], self.mask_t.ap(), self.dST)
        self.static_tok = self.dST
        fns = {0: self.p0, 1: self.p1, 2: lambda: self.p_out(0), 3: lambda: self.p_ffn(0),
               4: self.p4, 5: self.p5, 6: lambda: self.p_out(1), 7: lambda: self.p_ffn(1)}
        self._wslot_tok = [None, None]
        for ph in self.phases:
            fns[ph]()
            self.barrier()
        em.check()
        em.replay()
        return self.nc



def host_prep(inputs):
    f = lambda k: np.ascontiguousarray(np.asarray(inputs[k], dtype=np.float32))
    pc = lambda a: np.ascontiguousarray(a.reshape(16, 128).T)
    w = {}
    w["wq0"] = f("L0_attn_Wq"); w["bq0"] = f("L0_attn_bq")
    w["wk0"] = f("L0_attn_Wk")
    w["bk0"] = f("L0_attn_bk")
    w["wv0"] = f("L0_attn_Wv"); w["bv0"] = f("L0_attn_bv")
    w["wo0"] = f("L0_attn_Wo"); w["bo0"] = f("L0_attn_bo")
    w["gw0"] = f("L0_ffn_gate_w")
    w["wg0"] = f("L0_ffn_w_gate"); w["wu0"] = f("L0_ffn_w_up"); w["wd0"] = f("L0_ffn_w_down")
    w["swg0"] = f("L0_ffn_sw_gate"); w["swu0"] = f("L0_ffn_sw_up"); w["swd0"] = f("L0_ffn_sw_down")
    perm = l1_head_perm()
    w["wq1"] = np.ascontiguousarray(f("L1_attn_wq")[:, perm])
    w["bq1"] = np.ascontiguousarray(f("L1_attn_bq")[perm])
    w["wk1"] = np.ascontiguousarray(f("L1_attn_wk")[:, perm])
    w["bk1"] = np.ascontiguousarray(f("L1_attn_bk")[perm])
    w["wv1"] = f("L1_attn_wv"); w["bv1"] = f("L1_attn_bv")
    w["wo1"] = f("L1_attn_wo"); w["bo1"] = f("L1_attn_bo")
    w["gw1"] = np.ascontiguousarray(f("L1_ffn_gate_w").T)
    w["rmsw"] = pc(f("L1_attn_rms_w"))
    w["w1"] = f("L1_ffn_w1"); w["w2"] = f("L1_ffn_w2")
    w["sw1"] = np.ascontiguousarray(f("L1_ffn_sw1")[0])
    w["sw2"] = np.ascontiguousarray(f("L1_ffn_sw2")[0])
    return w



from concourse.bass_utils import run_bass_kernel_spmd

_PROG_CACHE = {}


def _get_prog(group):
    if group not in _PROG_CACHE:
        m = MK(group)
        nc = m.build()
        _PROG_CACHE[group] = (m, nc)
    return _PROG_CACHE[group]


def kernel(**inputs):
    w = host_prep(inputs)
    x_all = np.ascontiguousarray(np.asarray(inputs["L0_attn_x"], dtype=np.float32))
    ncores = 8
    m, nc = _get_prog("ALL")
    in_maps = []
    for c in range(ncores):
        dmap = {}
        for n in m.ext_in:
            if n == "x":
                dmap[n] = np.ascontiguousarray(x_all[c])
            else:
                dmap[n] = w[n]
        in_maps.append(dmap)
    res = run_bass_kernel_spmd(nc, in_maps, core_ids=list(range(ncores)))
    out = np.stack([np.asarray(res.results[c]["out"], dtype=np.float32) for c in range(ncores)], axis=0)
    return out


# version 06
# speedup vs baseline: 1.0931x; 1.0931x over previous
import numpy as np
import ml_dtypes
import concourse.bass as bass
import concourse.mybir as mybir

F32 = mybir.dt.float32
BF16 = mybir.dt.bfloat16
AF = mybir.ActivationFunctionType
ALU = mybir.AluOpType
AX = mybir.AxisListType

D = 2048
T = 2048
KC = D // 128
MT = T // 128
NEG = -1e30
EPS_LN = 1e-5

ENGINES = ("tensor", "vector", "scalar", "gpsimd", "sync")


class Sem:
    __slots__ = ("h", "n", "name")

    def __init__(self, nc, name):
        self.h = nc.alloc_semaphore(name)
        self.n = 0
        self.name = name


class Em:

    def __init__(self, nc):
        self.nc = nc
        self.streams = {e: [] for e in ENGINES}
        self.sems = {}

    def sem(self, name):
        if name not in self.sems:
            self.sems[name] = Sem(self.nc, name)
        return self.sems[name]

    def emit(self, eng, fn, wait=(), inc=None, inc_k=1):
        waits = []
        for w in wait:
            if isinstance(w, Sem):
                waits.append((w, w.n))
            else:
                waits.append((w[0], w[1]))
        tok = None
        if inc is not None:
            inc.n += inc_k
            tok = (inc, inc.n)
        self.streams[eng].append((fn, waits, inc, inc_k))
        return tok

    def dma(self, eng, out_ap, in_ap, sem, wait=(), **kw):
        return self.emit(eng, lambda e: e.dma_start(out_ap, in_ap, **kw),
                         wait=wait, inc=sem, inc_k=16)

    def check(self):
        pos = {e: 0 for e in ENGINES}
        counts = {}
        progressed = True
        total = sum(len(s) for s in self.streams.values())
        done = 0
        while progressed:
            progressed = False
            for e in ENGINES:
                while pos[e] < len(self.streams[e]):
                    fn, waits, inc, inc_k = self.streams[e][pos[e]]
                    if all(counts.get(s.name, 0) >= v for s, v in waits):
                        if inc is not None:
                            counts[inc.name] = counts.get(inc.name, 0) + inc_k
                        pos[e] += 1
                        done += 1
                        progressed = True
                    else:
                        break
        if done != total:
            for e in ENGINES:
                if pos[e] < len(self.streams[e]):
                    fn, waits, inc, inc_k = self.streams[e][pos[e]]
                    missing = [(s.name, v, counts.get(s.name, 0)) for s, v in waits
                               if counts.get(s.name, 0) < v]
                    print(f"DEADLOCK {e}@{pos[e]}/{len(self.streams[e])}: needs {missing}")
            raise RuntimeError(f"deadlock: {done}/{total} instructions schedulable")
        print(f"schedulability ok: {total} instructions")

    def replay(self):
        nc = self.nc
        with nc.Block() as blk:
            for eng_name in ENGINES:
                stream = self.streams[eng_name]

                def make(stream):
                    def run(eng):
                        hi = {}
                        for fn, waits, inc, inc_k in stream:
                            byname = {}
                            for s, v in waits:
                                if v > 0 and (s.name not in byname or byname[s.name][1] < v):
                                    byname[s.name] = (s, v)
                            for s, v in byname.values():
                                if hi.get(s.name, 0) < v:
                                    eng.wait_ge(s.h, v)
                                    hi[s.name] = v
                            ins = fn(eng)
                            if inc is not None:
                                ins.then_inc(inc.h, inc_k)
                    return run

                getattr(blk, eng_name)(make(stream))


def bfbits(a):
    return np.ascontiguousarray(a.astype(ml_dtypes.bfloat16)).view(np.uint16)



def l0_rope_tables(scale_q):
    j = np.arange(32, dtype=np.float64)
    freqs = 10000.0 ** (-2.0 * j / 64.0)
    pos = np.arange(1, T + 1, dtype=np.float64)[None, :]
    th = freqs[:, None] * pos
    c = np.cos(th)
    s = np.sin(th)
    CC = np.concatenate([c, c, c, c], axis=0)
    SS = np.concatenate([-s, s, -s, s], axis=0)
    return (CC * scale_q).astype(np.float32), (SS * scale_q).astype(np.float32)


def l1_rope_tables(scale_q):
    k = np.arange(64, dtype=np.float64)
    freqs = 10000.0 ** (-2.0 * k / 128.0)
    pos = np.arange(0, T, dtype=np.float64)[None, :]
    th = freqs[:, None] * pos
    c = np.cos(th)
    s = np.sin(th)
    even = (np.arange(64) % 2 == 0)[:, None]
    A = np.where(even, c, s)
    B = np.where(even, -s, c)
    A2 = np.concatenate([A, A], axis=0)
    B2 = np.concatenate([B, B], axis=0)
    return (A2 * scale_q).astype(np.float32), (B2 * scale_q).astype(np.float32)


def l0_perm_matrix():
    P = np.zeros((128, 128), dtype=np.float32)
    for b in range(2):
        for i in range(32):
            P[b * 64 + i, b * 64 + 32 + i] = 1.0
            P[b * 64 + 32 + i, b * 64 + i] = 1.0
    return P


def l1_head_perm():
    perm = np.zeros(D, dtype=np.int64)
    for h in range(16):
        base = h * 128
        perm[base:base + 64] = base + np.arange(0, 128, 2)
        perm[base + 64:base + 128] = base + np.arange(1, 128, 2)
    return perm


def causal_mask_tile():
    return np.where(np.arange(128)[None, :] <= np.arange(128)[:, None],
                    0.0, NEG).astype(np.float32)


PARTBYTES = 229376
B0 = 17536
R1_OFF = B0
R2_OFF = B0 + 65536
CC_OFF = B0 + 131072
TS_OFF = B0 + 155648
WW_OFF = B0 + 172032
SM_OFF = B0 + 204800
FWC_OFF = CC_OFF
FWA_OFF = CC_OFF + 46080
FWB_OFF = FWA_OFF + 12288

SQ0 = 1.0 / np.sqrt(64.0)
SQ1 = 2.0 / np.sqrt(128.0)
GROUP_PHASES = {"A": (0, 1, 2), "B": (3,), "C": (4, 5, 6), "D": (7,),
                "P0": (0,), "P01": (0, 1), "P4": (4,), "ALL": (0, 1, 2, 3, 4, 5, 6, 7)}


class MK:
    def __init__(self, group="ALL"):
        self.group = group
        self.phases = GROUP_PHASES[group]
        nc = bass.Bass("TRN2", target_bir_lowering=False, debug=False)
        self.nc = nc
        self.em = Em(nc)
        self.ext_in = []
        self.ext_out = []
        self._mk_dram()
        self._mk_sbuf()
        self._mk_psum()
        self._mk_sems()

    def _mk_dram(self):
        nc = self.nc
        g = self.group

        def ext(name, shape, dt):
            t = nc.dram_tensor(name, shape, dt, kind="ExternalInput")
            self.ext_in.append(name)
            return t.ap()

        def extout(name, shape, dt):
            t = nc.dram_tensor(name, shape, dt, kind="ExternalOutput")
            self.ext_out.append(name)
            return t.ap()

        def internal(name, shape, dt):
            return nc.dram_tensor(name, shape, dt, kind="Internal").ap()

        def bound(name, shape, dt, produced, consumed):
            if g == "ALL":
                return internal(name, shape, dt)
            if g == produced or (produced == "A" and g in ("P0", "P01")):
                return extout(name, shape, dt)
            if g == consumed or (consumed == "C" and g == "P4"):
                return ext(name, shape, dt)
            return None

        d = self.d = {}
        ing = {"A": ["x", "wq0", "bq0", "wk0", "bk0", "wv0", "bv0", "wo0", "bo0", "gw0"],
               "P0": ["x", "wq0", "bq0", "wk0", "bk0", "wv0", "bv0"],
               "P01": ["x", "wq0", "bq0", "wk0", "bk0", "wv0", "bv0"],
               "P4": ["wq1", "bq1", "wk1", "bk1", "wv1", "bv1", "rmsw"],
               "B": ["wg0", "wu0", "wd0", "swg0", "swu0", "swd0"],
               "C": ["wq1", "bq1", "wk1", "bk1", "wv1", "bv1", "wo1", "bo1", "gw1", "rmsw"],
               "D": ["w1", "w2", "sw1", "sw2"]}
        shapes = {
            "x": ([T, D], F32), "wq0": ([D, D], F32), "bq0": ([D], F32),
            "wk0": ([D, 512], F32), "bk0": ([512], F32), "wv0": ([D, 512], F32),
            "bv0": ([512], F32), "wo0": ([D, D], F32), "bo0": ([D], F32),
            "gw0": ([D, 8], F32),
            "wg0": ([8, D, 1408], F32), "wu0": ([8, D, 1408], F32), "wd0": ([8, 1408, D], F32),
            "swg0": ([D, 2816], F32), "swu0": ([D, 2816], F32), "swd0": ([2816, D], F32),
            "wq1": ([D, D], F32), "bq1": ([D], F32), "wk1": ([D, D], F32),
            "bk1": ([D], F32), "wv1": ([D, D], F32), "bv1": ([D], F32),
            "wo1": ([D, D], F32), "bo1": ([D], F32), "gw1": ([D, 16], F32),
            "rmsw": ([128, 16], F32),
            "w1": ([16, D, 1408], F32), "w2": ([16, 1408, D], F32),
            "sw1": ([D, 1408], F32), "sw2": ([1408, D], F32),
        }
        for grp in ("A", "B", "C", "D"):
            for n in ing[grp]:
                if g == "ALL" or g == grp:
                    d[n] = ext(n, *shapes[n])
        if g in ("P0", "P01", "P4"):
            for n in ing[g]:
                if n not in d:
                    d[n] = ext(n, *shapes[n])
        d["xA"] = bound("xA", [T, D], F32, "A", "B")
        d["xnA"] = bound("xnA", [D, T], BF16, "A", "B")
        d["cmbA"] = bound("cmbA", [128, MT * 10], F32, "A", "B")
        d["xB"] = bound("xB", [T, D], F32, "B", "C")
        d["xC"] = bound("xC", [T, D], F32, "C", "D")
        d["xnC"] = bound("xnC", [D, T], BF16, "C", "D")
        d["cmbC"] = bound("cmbC", [128, MT * 17], F32, "C", "D")
        if g in ("ALL", "D"):
            d["out"] = extout("out", [T, D], F32)
        if g in ("ALL", "A", "C", "P0", "P01", "P4"):
            d["k_hbm"] = internal("k_hbm", [512, T], BF16)
            d["v_hbm"] = internal("v_hbm", [T, D], BF16)
            d["y_hbm"] = internal("y_hbm", [T, D], BF16)
        if g in ("ALL", "C", "P4"):
            d["r_hbm"] = internal("r_hbm", [T, D], F32)

        self.ident_t = nc.inline_tensor(bfbits(np.eye(128, dtype=np.float32)), name="c_ident")
        self.ones_t = nc.inline_tensor(bfbits(np.ones((1, 512), dtype=np.float32)), name="c_ones")
        self.perm_t = nc.inline_tensor(bfbits(l0_perm_matrix()), name="c_perm")
        P1m = np.zeros((128, 128), dtype=np.float32)
        for i in range(64):
            P1m[i, 64 + i] = 1.0
            P1m[64 + i, i] = 1.0
        self.perm1_t = nc.inline_tensor(bfbits(P1m), name="c_perm1")
        self.mask_t = nc.inline_tensor(causal_mask_tile(), name="c_mask")
        cc0, ss0 = l0_rope_tables(SQ0)
        cc0u, ss0u = l0_rope_tables(1.0)
        a1, b1 = l1_rope_tables(SQ1)
        a1u, b1u = l1_rope_tables(1.0)
        self.tbl0_t = nc.inline_tensor(np.concatenate(
            [bfbits(cc0), bfbits(ss0), bfbits(cc0u), bfbits(ss0u)], axis=1), name="c_tbl0")
        self.tbl1_t = nc.inline_tensor(np.concatenate(
            [bfbits(a1), bfbits(b1), bfbits(a1u), bfbits(b1u)], axis=1), name="c_tbl1")

    def _mk_sbuf(self):
        nc = self.nc
        at = nc.alloc_sbuf_tensor_at
        self.R1 = at("R1", [128, 32768], BF16, offset=R1_OFF)
        self.R2 = at("R2", [128, 32768], BF16, offset=R2_OFF)
        self.ACC = at("ACCv", [128, 16384], F32, offset=R2_OFF)
        self.CCR = at("CCR", [128, 12288], BF16, offset=CC_OFF)
        self.WA = at("WAb", [128, 8192], BF16, offset=WW_OFF)
        self.WB = at("WBb", [128, 8192], BF16, offset=WW_OFF + 16384)
        self.SCF = [at("SCFa", [128, 2048], F32, offset=WW_OFF),
                    at("SCFb", [128, 2048], F32, offset=WW_OFF + 16384)]
        self.TSa = at("TSa", [128, 2048], F32, offset=TS_OFF)
        self.TSb = at("TSb", [128, 2048], F32, offset=TS_OFF + 8192)
        self.TMPF = at("TMPF", [128, 1024], F32, offset=TS_OFF + 8192)
        self.WC = at("WCb", [128, 23040], BF16, offset=FWC_OFF)
        self.FWA = at("FWAb", [128, 6144], BF16, offset=FWA_OFF)
        self.FWB = at("FWBb", [128, 6144], BF16, offset=FWB_OFF)
        o = SM_OFF

        def sm(name, shape, dt, nbytes):
            nonlocal o
            t = at(name, shape, dt, offset=o)
            o += nbytes
            return t
        self.ST = sm("ST", [128, 96], F32, 384)
        self.CMB = sm("CMB", [128, MT * 18], F32, MT * 18 * 4)
        self.IDT = sm("IDT", [128, 128], BF16, 256)
        self.PERM = sm("PERM", [128, 128], BF16, 256)
        self.PERM1 = sm("PERM1", [128, 128], BF16, 256)
        self.ONESB = sm("ONESB", [128, 512], BF16, 1024)
        self.MASK = sm("MASK", [128, 128], F32, 512)
        self.RMW = sm("RMW", [128, 16], F32, 64)
        self.BROW = sm("BROW", [128, 1536], BF16, 3072)
        assert o <= PARTBYTES, f"SM overflow {o}"

    def _mk_psum(self):
        nc = self.nc
        self.PS = nc.alloc_psum_tensor("PS", [128, 2048], F32)
        self.PA = nc.alloc_psum_tensor("PA", [128, 512], F32)
        self.PBb = nc.alloc_psum_tensor("PBb", [128, 1024], BF16)
        self.PC = nc.alloc_psum_tensor("PC", [128, 512], F32)
        self.PDB = nc.alloc_psum_tensor("PDB", [128, 1024], BF16)

    def _mk_sems(self):
        s = self.em.sem
        self.pe = s("pe")
        self.ve = s("ve")
        self.se = s("se")
        self.dW = [s("dW0"), s("dW1"), s("dW2")]
        self.dX = [s("dX0"), s("dX1")]
        self.dS = [s("dS0"), s("dS1")]
        self.dR1 = s("dR1")
        self.dR2 = s("dR2")
        self.dST = s("dST")
        self.dB = [s("dB0"), s("dB1"), s("dB2")]
        self.dOUT = s("dOUT")

    def barrier(self, extra=()):
        em = self.em
        sems = [self.pe, self.ve, self.se, self.dR1, self.dR2, self.dST, self.dOUT] + \
               self.dW + self.dX + self.dS + self.dB + list(extra)
        toks = [(s, s.n) for s in sems if s.n > 0]
        for eng in ENGINES:
            em.emit(eng, lambda e: e.nop(), wait=toks)


    def ln_tile(self, x_slot, t_x, st_base, out_slot, rms=False, ln_f32_out=None):
        em = self.em
        ST = self.ST
        c = lambda i: ST[:, st_base + i:st_base + i + 1]
        t_sum = em.emit("vector", lambda v: v.reduce_sum(c(0), x_slot[:, :], axis=AX.X, negate=True),
                        wait=[t_x], inc=self.ve)
        t_sq = em.emit("scalar", lambda s: s.activation(out_slot[:, :], x_slot[:, :], AF.Square,
                                                        accum_out=c(1)), wait=[t_x], inc=self.se)
        t0 = em.emit("vector", lambda v: v.tensor_scalar(c(2), c(0), 1.0 / D, None, op0=ALU.mult),
                     wait=[t_sum], inc=self.ve)
        t1 = em.emit("vector", lambda v: v.tensor_mul(c(3), c(2), c(2)), wait=[t0], inc=self.ve)
        t2 = em.emit("vector", lambda v: v.tensor_scalar(c(4), c(1), 1.0 / D, EPS_LN,
                                                         op0=ALU.mult, op1=ALU.add),
                     wait=[t_sq, t1], inc=self.ve)
        t3 = em.emit("vector", lambda v: v.tensor_sub(c(4), c(4), c(3)), wait=[t2], inc=self.ve)
        t4 = em.emit("scalar", lambda s: s.activation(c(5), c(4), AF.Sqrt), wait=[t3], inc=self.se)
        t5 = em.emit("vector", lambda v: v.reciprocal(c(6), c(5)), wait=[t4], inc=self.ve)
        if not rms:
            return em.emit("vector", lambda v: v.tensor_scalar(out_slot[:, :], x_slot[:, :],
                                                               c(2), c(6), op0=ALU.add, op1=ALU.mult),
                           wait=[t5], inc=self.ve)
        t7 = em.emit("vector", lambda v: v.tensor_scalar(ln_f32_out[:, :], x_slot[:, :],
                                                         c(2), c(6), op0=ALU.add, op1=ALU.mult),
                     wait=[t5], inc=self.ve)
        t8 = em.emit("scalar", lambda s: s.activation(out_slot[:, :], ln_f32_out[:, :], AF.Square,
                                                      accum_out=c(1)), wait=[t7], inc=self.se)
        t9 = em.emit("vector", lambda v: v.tensor_scalar(c(4), c(1), 1.0 / D, EPS_LN,
                                                         op0=ALU.mult, op1=ALU.add),
                     wait=[t8], inc=self.ve)
        ta = em.emit("scalar", lambda s: s.activation(c(5), c(4), AF.Sqrt), wait=[t9], inc=self.se)
        tb = em.emit("vector", lambda v: v.reciprocal(c(6), c(5)), wait=[ta], inc=self.ve)
        return em.emit("scalar", lambda s: s.activation(out_slot[:, :], ln_f32_out[:, :], AF.Copy,
                                                        scale=c(6)), wait=[tb], inc=self.se), t7

    def transpose_tile(self, src_slot, mt, dest, tok_src):
        em = self.em
        last = None
        for cch in range(16):
            pp = cch % 2
            g = self._tt_guard[pp]
            tpe = em.emit("tensor",
                          (lambda cch, pp: lambda t: t.transpose(
                              [self.PBb, self.PDB][pp][:, 0:128],
                              src_slot[:, cch * 128:(cch + 1) * 128], self.IDT[:, :]))(cch, pp),
                          wait=[tok_src, self.static_tok] + ([g] if g else []), inc=self.pe)
            tcp = em.emit("vector",
                          (lambda cch, pp, mt: lambda v: v.tensor_copy(
                              dest[:, cch * 2048 + mt * 128:cch * 2048 + mt * 128 + 128],
                              [self.PBb, self.PDB][pp][:, 0:128]))(cch, pp, mt),
                          wait=[tpe], inc=self.ve)
            self._tt_guard[pp] = tcp
            last = tcp
        return last

    def dma_weight_unit(self, wslot_idx, dram_ap, cols0, width, wait=(), ffn=False):
        em = self.em
        slot = ([self.FWA, self.FWB] if ffn else [self.WA, self.WB])[wslot_idx]
        sem = self.dW[wslot_idx]
        src = dram_ap[:, cols0:cols0 + width].rearrange("(k p) f -> p k f", p=128)
        dst = slot[:, 0:16 * width].rearrange("p (k f) -> p k f", f=width)
        tok = em.dma("gpsimd", dst, src, sem, wait=wait)
        return tok, slot

    def rms_scale_unit(self, slot, width, t_w, t_rm):
        em = self.em
        tt = None
        for k in range(16):
            tt = em.emit("vector",
                         (lambda k: lambda v: v.tensor_scalar(
                             slot[:, k * width:(k + 1) * width], slot[:, k * width:(k + 1) * width],
                             self.RMW[:, k:k + 1], None, op0=ALU.mult))(k),
                         wait=[t_w, t_rm] if k == 0 else [t_w], inc=self.ve)
        return tt

    def bias_row(self, roff, dram_vec, lo, width):
        g = self._brow_tok.get(roff)
        tok = self.em.dma("gpsimd", self.BROW[0:1, roff:roff + width],
                          dram_vec[lo:lo + width][None, :], self.dB[roff // 512],
                          wait=[g] if g else [])
        return tok


    def p0(self):
        em, d = self.em, self.d
        XN, QD = self.R1, self.R2
        TBL = self.CCR
        SLc = [self.CCR[:, 8192:10240], self.CCR[:, 10240:12288]]
        self._tt_guard = [None, None]
        self._wslot_tok = [None, None]
        self._rg = {}

        t_tbl = em.dma("sync", TBL[:, 0:8192], self.tbl0_t.ap().bitcast(BF16), self.dST)
        self._brow_tok = {}

        import os as _os
        _nt = int(_os.environ.get("MK_LN_TILES", MT))
        slot_tok = None
        t_tr = None
        for mt in range(_nt):
            sl = mt % 2
            ts = self.TSa
            w = [slot_tok] if slot_tok else []
            t_x = em.dma("sync", ts[:, :], d["x"][mt * 128:(mt + 1) * 128, :], self.dX[0], wait=w)
            t_xn = self.ln_tile(ts, t_x, 0, SLc[sl])
            slot_tok = t_xn
            t_tr = self.transpose_tile(SLc[sl], mt, XN, t_xn)
        xn_tok = t_tr
        import os
        if os.environ.get("MK_STOP") == "ln":
            return

        def proj_units(wdram, nunits, out_emit, boff, bias_vec, rms=False, t_rm=None):
            wtok = self._wslot_tok
            pp_tok = [None, None]
            for u in range(nunits):
                wsl = u % 2
                w = [wtok[wsl]] if wtok[wsl] else []
                t_w, slot = self.dma_weight_unit(wsl, wdram, u * 512, 512, wait=w)
                t_row = self.bias_row(boff, bias_vec, u * 512, 512)
                if rms:
                    t_w = self.rms_scale_unit(slot, 512, t_w, t_rm)
                t_last = None
                for m in range(4):
                    for n in range(4):
                        pp = (m * 4 + n) % 2
                        ps = [self.PA, self.PC][pp]
                        wl = [t_w, t_row, xn_tok, t_tbl, self.static_tok]
                        if pp_tok[pp]:
                            wl.append(pp_tok[pp])
                        t_pre = em.emit("tensor",
                                (lambda m, ps, boff: lambda t: t.matmul(
                                    ps[:, :],
                                    self.BROW[0:1, boff + m * 128:boff + (m + 1) * 128],
                                    self.ONESB[0:1, 0:512],
                                    start=True, stop=False))(m, ps, boff),
                                wait=wl, inc=self.pe)
                        self._brow_tok[boff] = t_pre
                        for k in range(16):
                            t_mm = em.emit("tensor",
                                           (lambda k, m, n, ps, slot: lambda t: t.matmul(
                                               ps[:, :],
                                               slot[:, k * 512 + m * 128:k * 512 + (m + 1) * 128],
                                               XN[:, k * 2048 + n * 512:k * 2048 + (n + 1) * 512],
                                               start=False, stop=(k == 15)))(k, m, n, ps, slot),
                                           inc=self.pe if k == 15 else None)
                        pp_tok[pp] = out_emit(u * 4 + m, n, ps, t_mm)
                        t_last = t_mm
                wtok[wsl] = t_last
            return pp_tok


        def q_out(ch, n, ps, t_mm):
            sl_q = lambda: QD[:, ch * 2048 + n * 512:ch * 2048 + (n + 1) * 512]
            t_ev = em.emit("scalar",
                           (lambda ch, n, ps: lambda s: s.activation(
                               QD[:, ch * 2048 + n * 512:ch * 2048 + (n + 1) * 512],
                               ps[:, :], AF.Copy))(ch, n, ps),
                           wait=[t_mm], inc=self.se)
            return self.rope_l0(QD, ch, n, t_ev, 0, self._rg)

        proj_units(d["wq0"], 4, q_out, 0, d["bq0"])
        if os.environ.get("MK_STOP") == "q":
            return

        kdone = {}
        kslot_tok = [None, None]

        def k_out(ch, n, ps, t_mm):
            ksl = SLc[ch % 2]
            wl = [t_mm]
            if n == 0 and kslot_tok[ch % 2]:
                wl.append(kslot_tok[ch % 2])
            t_ev = em.emit("scalar",
                           (lambda ch, n, ps, ksl: lambda s: s.activation(
                               ksl[:, n * 512:(n + 1) * 512], ps[:, :], AF.Copy))(ch, n, ps, ksl),
                           wait=wl, inc=self.se)
            t_rope = self.rope_l0(ksl, 0, n, t_ev, 4096, self._rg)
            kdone.setdefault(ch, []).append(t_rope)
            if n == 3:
                toks = kdone[ch]
                t_dma = em.dma("sync", self.d["k_hbm"][ch * 128:(ch + 1) * 128, :],
                               ksl[:, :], self.dS[ch % 2], wait=toks)
                kslot_tok[ch % 2] = t_dma
            return t_rope

        proj_units(d["wk0"], 1, k_out, 512, d["bk0"])
        if os.environ.get("MK_STOP") == "k":
            return

        wv_wait = [self._wslot_tok[0]] if self._wslot_tok[0] else []
        t_wv, vslot = self.dma_weight_unit(0, d["wv0"], 0, 512, wait=wv_wait)
        t_bv = self.bias_row(1024, d["bv0"], 0, 512)
        vsl_tok = [kslot_tok[0], kslot_tok[1]]
        pp_tok = [None, None]
        for mt in range(MT):
            pp = mt % 2
            ps = [self.PA, self.PC][pp]
            wl = [t_wv, xn_tok, t_bv, self.static_tok]
            if pp_tok[pp]:
                wl.append(pp_tok[pp])
            t_pre = em.emit("tensor",
                            (lambda ps: lambda t: t.matmul(
                                ps[:, :], self.ONESB[0:1, 0:128], self.BROW[0:1, 1024:1536],
                                start=True, stop=False))(ps),
                            wait=wl, inc=self.pe)
            self._brow_tok[1024] = t_pre
            for k in range(16):
                t_mm = em.emit("tensor",
                               (lambda k, mt, ps: lambda t: t.matmul(
                                   ps[:, :], XN[:, k * 2048 + mt * 128:k * 2048 + mt * 128 + 128],
                                   vslot[:, k * 512:(k + 1) * 512],
                                   start=False, stop=(k == 15)))(k, mt, ps),
                               inc=self.pe if k == 15 else None)
            vsl = SLc[pp]
            wl2 = [t_mm]
            if vsl_tok[pp]:
                wl2.append(vsl_tok[pp])
            t_ev = em.emit("scalar",
                           (lambda vsl, ps: lambda s: s.activation(
                               vsl[:, 0:512], ps[:, :], AF.Copy))(vsl, ps),
                           wait=wl2, inc=self.se)
            t_dma = em.dma("sync", d["v_hbm"][mt * 128:(mt + 1) * 128, 0:512],
                           vsl[:, 0:512], self.dS[pp], wait=[t_ev])
            vsl_tok[pp] = t_dma
            pp_tok[pp] = t_ev

    def rope_l0(self, dst, ch, n, t_ev, tbl_off, guard):
        em = self.em
        TBL = self.CCR
        sp = lambda: dst[:, ch * 2048 + n * 512:ch * 2048 + (n + 1) * 512]
        TMP = self.TMPF[:, (n % 2) * 512:(n % 2) * 512 + 512]
        g = guard.get(("tmp", n % 2))
        t_sw = em.emit("tensor",
                       (lambda ch, n: lambda t: t.matmul(
                           self.PS[:, 0:512], self.PERM[:, :],
                           dst[:, ch * 2048 + n * 512:ch * 2048 + (n + 1) * 512],
                           start=True, stop=True))(ch, n),
                       wait=[t_ev, self.static_tok] + ([guard["pd"]] if "pd" in guard else []), inc=self.pe)
        t1 = em.emit("vector",
                     (lambda n, TMP: lambda v: v.tensor_mul(
                         TMP[:, :], self.PS[:, 0:512],
                         TBL[:, tbl_off + 2048 + n * 512:tbl_off + 2048 + (n + 1) * 512]))(n, TMP),
                     wait=[t_sw] + ([g] if g else []), inc=self.ve)
        guard["pd"] = t1
        t2 = em.emit("vector",
                     (lambda ch, n: lambda v: v.tensor_mul(
                         dst[:, ch * 2048 + n * 512:ch * 2048 + (n + 1) * 512],
                         dst[:, ch * 2048 + n * 512:ch * 2048 + (n + 1) * 512],
                         TBL[:, tbl_off + n * 512:tbl_off + (n + 1) * 512]))(ch, n),
                     wait=[t_sw], inc=self.ve)
        t3 = em.emit("vector",
                     (lambda ch, n, TMP: lambda v: v.tensor_add(
                         dst[:, ch * 2048 + n * 512:ch * 2048 + (n + 1) * 512],
                         dst[:, ch * 2048 + n * 512:ch * 2048 + (n + 1) * 512],
                         TMP[:, :]))(ch, n, TMP),
                     wait=[t1, t2], inc=self.ve)
        guard[("tmp", n % 2)] = t3
        return t3


    def p1(self):
        em, d = self.em, self.d
        QD, R1 = self.R2, self.R1
        CCR = self.CCR
        for s_ in range(8):
            em.dma("sync", R1[0:64, s_ * 2048:(s_ + 1) * 2048],
                   d["k_hbm"][s_ * 64:(s_ + 1) * 64, :], self.dR1)
            em.dma("sync", R1[64:128, s_ * 2048:(s_ + 1) * 2048],
                   d["k_hbm"][s_ * 64:(s_ + 1) * 64, :], self.dR1)
        t_k = (self.dR1, self.dR1.n)
        t_v = em.dma("sync", R1[:, 16384:24576].rearrange("p (j f) -> p j f", f=512),
                     d["v_hbm"][:, 0:512].rearrange("(j p) f -> p j f", p=128), self.dR1)
        self.attention(layer=0, t_kv=[t_k, t_v], QD=QD, KD=R1, R1=R1)

    def attention(self, layer, t_kv, QD, KD, R1):
        em, d = self.em, self.d
        CCR = self.CCR
        nheads = 32 if layer == 0 else 16
        dh = 64 if layer == 0 else 128
        ST = self.ST
        ysl_tok = [None, None]
        self._tt_guard = [None, None]
        ps_guard = []
        self._attg = [None, None]
        scf_guard = [None, None]
        pc_guard = None
        cp_flip = [0]
        pend = None
        for i in range(MT):
            W = (i + 1) * 128
            ysl = CCR[:, 8192 + (i % 2) * 2048:8192 + (i % 2) * 2048 + 2048]
            for h in range(nheads):
                hp = h % 2
                ATTs = CCR[:, hp * 2048:hp * 2048 + 2048]
                ATTTs = CCR[:, 4096 + hp * 2048:4096 + hp * 2048 + 2048]
                SCFs = self.SCF[hp]
                c0 = ST[:, 4 * hp:4 * hp + 1]
                c1 = ST[:, 4 * hp + 1:4 * hp + 2]
                c2 = ST[:, 4 * hp + 2:4 * hp + 3]
                if layer == 0:
                    qc, qr = h // 2, 64 * (h % 2)
                    kvh = h // 4
                    kc, kr = kvh, qr
                    vcol = (lambda kvh: lambda j: 16384 + j * 512 + kvh * 64)(kvh)
                else:
                    qc, qr = h // 2, 64 * (h % 2)
                    kc, kr = 8 + h // 2, 64 * (h % 2)
                    vcol = (lambda h: lambda j: j * 2048 + h * 128)(h)
                nspan = (W + 511) // 512
                wl0 = list(t_kv) + list(ps_guard)
                for s_ in range(nspan):
                    w0, w1 = s_ * 512, min(W, s_ * 512 + 512)
                    t_sc = em.emit("tensor",
                                   (lambda i, qc, qr, kc, kr, w0, w1: lambda t: t.matmul(
                                       self.PS[:, w0:w1],
                                       QD[qr:qr + 64, qc * 2048 + i * 128:qc * 2048 + (i + 1) * 128],
                                       KD[kr:kr + 64, kc * 2048 + w0:kc * 2048 + w1],
                                       start=True, stop=True))(i, qc, qr, kc, kr, w0, w1),
                                   wait=wl0 if s_ == 0 else (),
                                   inc=self.pe if s_ == nspan - 1 else None)
                if pend is not None:
                    pc_guard = self._attn_tail(**pend)
                t_mk = em.emit("vector",
                               (lambda i: lambda v: v.tensor_add(
                                   self.PS[:, i * 128:(i + 1) * 128],
                                   self.PS[:, i * 128:(i + 1) * 128], self.MASK[:, :]))(i),
                               wait=[t_sc, self.static_tok], inc=self.ve)
                t_mx = em.emit("vector",
                               (lambda W, c0: lambda v: v.reduce_max(
                                   c0, self.PS[:, 0:W], axis=AX.X, negate=True))(W, c0),
                               wait=[t_mk] + ([scf_guard[hp]] if scf_guard[hp] else []), inc=self.ve)
                wl_ex = [t_mx]
                if self._attg[hp]:
                    wl_ex.append(self._attg[hp])
                t_ex = em.emit("scalar",
                               (lambda W, ATTs, c0: lambda s: s.activation(
                                   ATTs[:, 0:W], self.PS[:, 0:W], AF.Exp, bias=c0))(W, ATTs, c0),
                               wait=wl_ex, inc=self.se)
                ps_guard = [t_ex]
                t_sm = em.emit("vector",
                               (lambda W, ATTs, c1: lambda v: v.reduce_sum(
                                   c1, ATTs[:, 0:W], axis=AX.X))(W, ATTs, c1),
                               wait=[t_ex], inc=self.ve)
                t_rc = em.emit("vector", (lambda c1, c2: lambda v: v.reciprocal(c2, c1))(c1, c2),
                               wait=[t_sm], inc=self.ve)
                pend = dict(i=i, h=h, hp=hp, ATTs=ATTs, ATTTs=ATTTs, t_ex=t_ex, t_rc=t_rc,
                            vcol=vcol, dh=dh, c2=c2, ysl=ysl, R1=R1,
                            y_wait=(ysl_tok[i % 2] if h == 0 else None), cp_flip=cp_flip,
                            pc_guard_in=pc_guard)
                if h == nheads - 1:
                    pc_guard = self._attn_tail(**pend)
                    pend = None
                    if True:
                        pass
            t_y = em.dma("sync", d["y_hbm"][i * 128:(i + 1) * 128, :], ysl[:, :],
                         self.dS[i % 2], wait=[self._last_yevac])
            ysl_tok[i % 2] = t_y

    def _attn_tail(self, i, h, hp, ATTs, ATTTs, t_ex, t_rc, vcol, dh, c2, ysl, R1,
                   y_wait, cp_flip, pc_guard_in):
        em = self.em
        t_cp = None
        cp_last = {}
        att_done = None
        for j in range(i + 1):
            pp = j % 2
            g = self._tt_guard[pp]
            t_tp = em.emit("tensor",
                           (lambda j, pp, ATTs: lambda t: t.transpose(
                               [self.PBb, self.PDB][pp][:, 0:128],
                               ATTs[:, j * 128:(j + 1) * 128], self.IDT[:, :]))(j, pp, ATTs),
                           wait=[t_ex, self.static_tok] + ([g] if g else []), inc=self.pe)
            att_done = t_tp
            eng = "vector" if (cp_flip[0] % 2 == 0) else "scalar"
            cp_flip[0] += 1
            t_cp = em.emit(eng,
                           (lambda j, pp, ATTTs: lambda e: e.tensor_copy(
                               ATTTs[:, j * 128:(j + 1) * 128],
                               [self.PBb, self.PDB][pp][:, 0:128]))(j, pp, ATTTs)
                           if eng == "vector" else
                           (lambda j, pp, ATTTs: lambda e: e.copy(
                               ATTTs[:, j * 128:(j + 1) * 128],
                               [self.PBb, self.PDB][pp][:, 0:128]))(j, pp, ATTTs),
                           wait=[t_tp], inc=self.ve if eng == "vector" else self.se)
            self._tt_guard[pp] = t_cp
            cp_last[eng] = t_cp
        self._attg[hp] = att_done
        wl = list(cp_last.values()) + ([pc_guard_in] if pc_guard_in else [])
        for j in range(i + 1):
            t_av = em.emit("tensor",
                           (lambda j, i, ATTTs, vcol, dh: lambda t: t.matmul(
                               self.PC[:, 0:dh],
                               ATTTs[:, j * 128:(j + 1) * 128],
                               R1[:, vcol(j):vcol(j) + dh],
                               start=(j == 0), stop=(j == i)))(j, i, ATTTs, vcol, dh),
                           wait=wl if j == 0 else (),
                           inc=self.pe if j == i else None)
        wl2 = [t_av, t_rc]
        if y_wait:
            wl2.append(y_wait)
        t_evy = em.emit("scalar",
                        (lambda h, ysl, dh, c2: lambda s: s.activation(
                            ysl[:, h * dh:(h + 1) * dh], self.PC[:, 0:dh],
                            AF.Copy, scale=c2))(h, ysl, dh, c2),
                        wait=wl2, inc=self.se)
        self._last_yevac = t_evy
        return t_evy


    def p_out(self, layer):
        em, d = self.em, self.d
        R1, R2 = self.R1, self.R2
        CCR = self.CCR
        self._tt_guard = [None, None]
        wo = d["wo0"] if layer == 0 else d["wo1"]
        bo_vec = d["bo0"] if layer == 0 else d["bo1"]
        self._brow_tok = {}
        t_wo = em.dma("gpsimd", R2[:, :].rearrange("p (k f) -> p k f", f=2048),
                      wo.rearrange("(k p) f -> p k f", p=128), self.dW[2])
        ysl_tok = [None, None]
        t_tr = None
        for mt in range(MT):
            sl = mt % 2
            ysl = CCR[:, sl * 2048:sl * 2048 + 2048]
            wly = [ysl_tok[sl]] if ysl_tok[sl] else []
            t_yl = em.dma("sync", ysl[:, :], d["y_hbm"][mt * 128:(mt + 1) * 128, :],
                          self.dS[sl], wait=wly)
            t_tr = self.transpose_tile(ysl, mt, R1, t_yl)
            ysl_tok[sl] = t_tr
        yfm_tok = t_tr
        x_src = d["x"] if layer == 0 else d["r_hbm"]
        x_dst = d["xA"] if layer == 0 else d["xC"]
        pp_tok = [None, None]
        for mt in range(MT):
            sl = mt % 2
            ts = [self.TSa, self.TSb][sl]
            t_x0 = em.dma("sync", ts[:, :], x_src[mt * 128:(mt + 1) * 128, :], self.dX[sl],
                          wait=[self.dOUT] if mt >= 2 else [])
            adds = []
            for u in range(4):
                pp = u % 2
                ps = [self.PA, self.PC][pp]
                t_bo = self.bias_row(1024, bo_vec, u * 512, 512)
                wl = [yfm_tok, t_wo, t_bo, self.static_tok]
                if pp_tok[pp]:
                    wl.append(pp_tok[pp])
                t_pre = em.emit("tensor",
                        (lambda u, ps: lambda t: t.matmul(
                            ps[:, :], self.ONESB[0:1, 0:128],
                            self.BROW[0:1, 1024:1536],
                            start=True, stop=False))(u, ps),
                        wait=wl, inc=self.pe)
                self._brow_tok[1024] = t_pre
                for k in range(16):
                    t_mm = em.emit("tensor",
                                   (lambda k, u, mt, ps: lambda t: t.matmul(
                                       ps[:, :], R1[:, k * 2048 + mt * 128:k * 2048 + mt * 128 + 128],
                                       R2[:, k * 2048 + u * 512:k * 2048 + (u + 1) * 512],
                                       start=False, stop=(k == 15)))(k, u, mt, ps),
                                   inc=self.pe if k == 15 else None)
                t_ad = em.emit("vector",
                               (lambda u, ts, ps: lambda v: v.tensor_add(
                                   ts[:, u * 512:(u + 1) * 512], ps[:, :],
                                   ts[:, u * 512:(u + 1) * 512]))(u, ts, ps),
                               wait=[t_mm, t_x0], inc=self.ve)
                pp_tok[pp] = t_ad
                adds.append(t_ad)
            t_st = em.dma("sync", x_dst[mt * 128:(mt + 1) * 128, :], ts[:, :], self.dOUT, wait=adds)
        self.p_out_b(layer)

    def p_out_b(self, layer):
        em, d = self.em, self.d
        R1 = self.R1
        CCR = self.CCR
        self._tt_guard = [None, None]
        x_src = d["xA"] if layer == 0 else d["xC"]
        xn_dst = d["xnA"] if layer == 0 else d["xnC"]
        cmb_dst = d["cmbA"] if layer == 0 else d["cmbC"]
        ne = 8 if layer == 0 else 16
        stride = 10 if layer == 0 else 17
        gw = d["gw0"] if layer == 0 else d["gw1"]
        wga = [self._wslot_tok[0]] if self._wslot_tok[0] else []
        t_gw = em.dma("gpsimd", self.WA[:, 0:16 * ne].rearrange("p (k f) -> p k f", f=ne),
                      gw.rearrange("(k p) f -> p k f", p=128), self.dW[0], wait=wga)
        slot_tok = [None, None]
        self._route_guard = None
        for mt in range(MT):
            sl = mt % 2
            ts = [self.TSa, self.TSb][sl]
            xsl = CCR[:, sl * 2048:sl * 2048 + 2048]
            wx = [self.dOUT] if mt < 2 else []
            if slot_tok[sl]:
                wx.append(slot_tok[sl])
            t_x1 = em.dma("sync", ts[:, :], x_src[mt * 128:(mt + 1) * 128, :], self.dX[sl], wait=wx)
            t_xn = self.ln_tile(ts, t_x1, 8 * sl, xsl)
            slot_tok[sl] = t_xn
            t_tr = self.transpose_tile(xsl, mt, R1, t_xn)
            for k in range(16):
                t_lg = em.emit("tensor",
                               (lambda k, mt, ne: lambda t: t.matmul(
                                   self.PC[:, 0:ne], R1[:, k * 2048 + mt * 128:k * 2048 + mt * 128 + 128],
                                   self.WA[:, k * ne:(k + 1) * ne],
                                   start=(k == 0), stop=(k == 15)))(k, mt, ne),
                               wait=([t_tr, t_gw] + ([self._route_guard] if self._route_guard else [])) if k == 0 else (),
                               inc=self.pe if k == 15 else None)
            self._route_guard = self.routing(layer, mt, t_lg, cmb_dst, stride, ne)
        last = None
        for kk in range(16):
            last = em.dma("sync", xn_dst[kk * 128:(kk + 1) * 128, :],
                          R1[:, kk * 2048:(kk + 1) * 2048], self.dOUT,
                          wait=[self._tt_guard[0], self._tt_guard[1], self._route_guard])
        em.dma("sync", cmb_dst, self.CMB[:, 0:MT * stride], self.dOUT, wait=[self._route_guard])

    def routing(self, layer, mt, t_lg, cmb_dst, stride, ne):
        em = self.em
        ST = self.ST
        SC = ST[:, 24:24 + ne]
        SEL = ST[:, 48:48 + ne]
        T2 = ST[:, 64:64 + ne]
        c = lambda i: ST[:, 16 + i:16 + i + 1]
        CMB = self.CMB
        if layer == 0:
            t0 = em.emit("vector", lambda v: v.reduce_max(c(0), self.PC[:, 0:ne], axis=AX.X, negate=True),
                         wait=[t_lg], inc=self.ve)
            t1 = em.emit("scalar", (lambda ne: lambda s: s.activation(SC[:, :], self.PC[:, 0:ne],
                                                                      AF.Exp, bias=c(0)))(ne),
                         wait=[t0], inc=self.se)
            t2 = em.emit("vector", lambda v: v.reduce_sum(c(1), SC[:, :], axis=AX.X), wait=[t1], inc=self.ve)
            t3 = em.emit("vector", lambda v: v.reciprocal(c(2), c(1)), wait=[t2], inc=self.ve)
            t4 = em.emit("vector", lambda v: v.tensor_scalar(SC[:, :], SC[:, :], c(2), None, op0=ALU.mult),
                         wait=[t3], inc=self.ve)
            tsc = t4
        else:
            tsc = em.emit("scalar", (lambda ne: lambda s: s.activation(SC[:, :], self.PC[:, 0:ne],
                                                                       AF.Sigmoid))(ne),
                          wait=[t_lg], inc=self.se)
            GS = ST[:, 40:44]
            gs_toks = []
            prev_te = None
            for g in range(4):
                Sg = SC[:, g * 4:(g + 1) * 4]
                ta = em.emit("vector", (lambda Sg: lambda v: v.reduce_max(c(0), Sg, axis=AX.X))(Sg),
                             wait=[tsc] + ([prev_te] if prev_te else []), inc=self.ve)
                tb = em.emit("vector", (lambda Sg, g: lambda v: v.tensor_scalar(
                    SEL[:, g * 4:(g + 1) * 4], Sg, c(0), None, op0=ALU.is_ge))(Sg, g),
                    wait=[ta], inc=self.ve)
                tc = em.emit("vector", (lambda Sg, g: lambda v: v.scalar_tensor_tensor(
                    T2[:, g * 4:(g + 1) * 4], SEL[:, g * 4:(g + 1) * 4], NEG, Sg,
                    op0=ALU.mult, op1=ALU.add))(Sg, g),
                    wait=[tb], inc=self.ve)
                td = em.emit("vector", (lambda g: lambda v: v.reduce_max(
                    c(1), T2[:, g * 4:(g + 1) * 4], axis=AX.X))(g),
                    wait=[tc], inc=self.ve)
                te = em.emit("vector", (lambda g: lambda v: v.tensor_add(GS[:, g:g + 1], c(0), c(1)))(g),
                             wait=[td], inc=self.ve)
                gs_toks.append(te)
                prev_te = te
            ta = em.emit("vector", lambda v: v.reduce_max(c(0), GS[:, :], axis=AX.X), wait=gs_toks, inc=self.ve)
            tb = em.emit("vector", lambda v: v.tensor_scalar(T2[:, 0:4], GS[:, :], c(0), None, op0=ALU.is_ge),
                         wait=[ta], inc=self.ve)
            tc = em.emit("vector", lambda v: v.scalar_tensor_tensor(T2[:, 4:8], T2[:, 0:4], NEG, GS[:, :],
                                                                   op0=ALU.mult, op1=ALU.add),
                         wait=[tb], inc=self.ve)
            td = em.emit("vector", lambda v: v.reduce_max(c(1), T2[:, 4:8], axis=AX.X), wait=[tc], inc=self.ve)
            tgs = em.emit("vector", lambda v: v.tensor_scalar(T2[:, 0:4], GS[:, :], c(1), None, op0=ALU.is_ge),
                          wait=[td], inc=self.ve)
            msk_toks = []
            for g in range(4):
                tm = em.emit("vector", (lambda g: lambda v: v.tensor_scalar(
                    SC[:, g * 4:(g + 1) * 4], SC[:, g * 4:(g + 1) * 4], T2[:, g:g + 1], None,
                    op0=ALU.mult))(g),
                    wait=[tgs], inc=self.ve)
                msk_toks.append(tm)
            tsc = msk_toks[-1]
            tsc = em.emit("vector", lambda v: v.tensor_copy(ST[:, 8:9], ST[:, 8:9]), wait=msk_toks, inc=self.ve)
        ta = em.emit("vector", (lambda ne: lambda v: v.reduce_max(c(3), SC[:, :], axis=AX.X))(ne),
                     wait=[tsc], inc=self.ve)
        tb = em.emit("vector", (lambda ne: lambda v: v.tensor_scalar(SEL[:, :], SC[:, :], c(3), None,
                                                                    op0=ALU.is_ge))(ne),
                     wait=[ta], inc=self.ve)
        tc = em.emit("vector", (lambda ne: lambda v: v.scalar_tensor_tensor(T2[:, 0:ne], SEL[:, :], NEG,
                                                                            SC[:, :], op0=ALU.mult,
                                                                            op1=ALU.add))(ne),
                     wait=[tb], inc=self.ve)
        td = em.emit("vector", (lambda ne: lambda v: v.reduce_max(c(4), T2[:, 0:ne], axis=AX.X))(ne),
                     wait=[tc], inc=self.ve)
        te = em.emit("vector", (lambda ne: lambda v: v.tensor_scalar(SEL[:, :], SC[:, :], c(4), None,
                                                                    op0=ALU.is_ge))(ne),
                     wait=[td], inc=self.ve)
        tw = em.emit("vector", (lambda ne: lambda v: v.tensor_mul(T2[:, 0:ne], SEL[:, :], SC[:, :]))(ne),
                     wait=[te], inc=self.ve)
        if layer == 0:
            out = em.emit("vector", (lambda mt, ne, stride: lambda v: v.tensor_copy(
                CMB[:, mt * stride:mt * stride + ne], T2[:, 0:ne]))(mt, ne, stride),
                wait=[tw], inc=self.ve)
        else:
            ts_ = em.emit("vector", (lambda ne: lambda v: v.reduce_sum(c(5), T2[:, 0:ne], axis=AX.X))(ne),
                          wait=[tw], inc=self.ve)
            ts2 = em.emit("vector", lambda v: v.tensor_scalar(c(5), c(5), 1e-20, None, op0=ALU.add),
                          wait=[ts_], inc=self.ve)
            ts3 = em.emit("vector", lambda v: v.reciprocal(c(6), c(5)), wait=[ts2], inc=self.ve)
            out = em.emit("vector", (lambda mt, ne, stride: lambda v: v.tensor_scalar(
                CMB[:, mt * stride:mt * stride + ne], T2[:, 0:ne], c(6), None, op0=ALU.mult))(mt, ne, stride),
                wait=[ts3], inc=self.ve)
        nones = stride - ne
        out = em.emit("vector", (lambda mt, ne, stride: lambda v: v.memset(
            CMB[:, mt * stride + ne:(mt + 1) * stride], 1.0))(mt, ne, stride),
            wait=[out], inc=self.ve)
        return out


    def p_ffn(self, layer):
        em, d = self.em, self.d
        R1, ACC, WC = self.R1, self.ACC, self.WC
        if layer == 0:
            units = [ (d["wg0"][e], d["wu0"][e], d["wd0"][e], e) for e in range(8) ]
            units.append((d["swg0"][:, 0:1408], d["swu0"][:, 0:1408], d["swd0"][0:1408, :], 8))
            units.append((d["swg0"][:, 1408:2816], d["swu0"][:, 1408:2816], d["swd0"][1408:2816, :], 9))
            stride = 10
            x_src, out_dst, xn_src, cmb_src = d["xA"], d["xB"], d["xnA"], d["cmbA"]
        else:
            units = [ (d["w1"][e], None, d["w2"][e], e) for e in range(16) ]
            units.append((d["sw1"], None, d["sw2"], 16))
            stride = 17
            x_src, out_dst, xn_src, cmb_src = d["xC"], d["out"], d["xnC"], d["cmbC"]
        import os
        if os.environ.get("MK_FFN_NE"):
            units = units[:int(os.environ["MK_FFN_NE"])]
        t_cmb = em.dma("sync", self.CMB[:, 0:MT * stride], cmb_src, self.dST)
        subw = [384, 384, 384, 256]
        sub0 = [0, 384, 768, 1152]
        subm = [3, 3, 3, 2]
        GOF = 16384
        USF = 27648
        wslot_tok = [None, None]
        wc_tok = None
        xnh_guard = None
        acc_store_toks = []
        ev_guard = [None, None]
        for half in (0, 1):
            accw = [self.dOUT] if acc_store_toks else []
            acc_toks = []
            for q in range(8):
                tk = em.dma("sync", ACC[:, q * 2048:(q + 1) * 2048],
                            x_src[(half * 8 + q) * 128:(half * 8 + q) * 128 + 128, :],
                            self.dR2, wait=accw if q == 0 else ())
                acc_toks.append(tk)
            t_acc = acc_toks[-1]
            xw = [xnh_guard] if xnh_guard else []
            for k in range(16):
                t_xn = em.dma("sync", R1[:, k * 1024:(k + 1) * 1024],
                              xn_src[k * 128:(k + 1) * 128, half * 1024:(half + 1) * 1024],
                              self.dR1, wait=xw if k == 0 else ())
            acc_store_toks = []
            last_umm = None
            for (wg_ap, wu_ap, wd_ap, ecol) in units:
                nslab = wd_ap.shape[0] // 128
                silu_toks = []
                g_toks = []
                prev_sub_gdone = None
                for sub in range(4):
                    w0, wid, nm = sub0[sub], subw[sub], subm[sub]
                    sub_silu = []
                    wsl = 0 if layer == 0 else (sub % 2)
                    wg_wait = [wslot_tok[wsl]] if wslot_tok[wsl] else []
                    t_wg, gslot = self.dma_weight_unit(wsl, wg_ap, w0, wid, wait=wg_wait, ffn=True)
                    pp_t = [None, None]
                    for m in range(nm):
                        for n in range(2):
                            pp = (m * 2 + n) % 2
                            ps = self.PS[:, pp * 512:(pp + 1) * 512]
                            wl = [t_wg, t_xn]
                            if m == 0 and n == 0 and prev_sub_gdone is not None:
                                wl.append(prev_sub_gdone)
                            if pp_t[pp]:
                                wl.append(pp_t[pp])
                            for k in range(16):
                                t_mm = em.emit("tensor",
                                               (lambda k, m, n, ps, gslot, wid: lambda t: t.matmul(
                                                   ps,
                                                   gslot[:, k * wid + m * 128:k * wid + (m + 1) * 128],
                                                   R1[:, k * 1024 + n * 512:k * 1024 + (n + 1) * 512],
                                                   start=(k == 0), stop=(k == 15)))(k, m, n, ps, gslot, wid),
                                               wait=wl if k == 0 else (),
                                               inc=self.pe if k == 15 else None)
                            last_umm = t_mm
                            if layer == 0:
                                dst = R1[:, USF + m * 1024 + n * 512:USF + m * 1024 + (n + 1) * 512]
                            else:
                                dst = R1[:, GOF + (sub0[sub] // 128 + m) * 1024 + n * 512:
                                         GOF + (sub0[sub] // 128 + m) * 1024 + (n + 1) * 512]
                            t_si = em.emit("scalar",
                                           (lambda ps, dst: lambda s: s.activation(dst, ps, AF.Silu))(ps, dst),
                                           wait=[t_mm], inc=self.se)
                            pp_t[pp] = t_si
                            silu_toks.append(t_si)
                            sub_silu.append(t_si)
                    wslot_tok[wsl] = last_umm
                    if layer == 0:
                        wsl2 = 1
                        wu_wait = [wslot_tok[wsl2]] if wslot_tok[wsl2] else []
                        t_wu, uslot = self.dma_weight_unit(wsl2, wu_ap, w0, wid, wait=wu_wait, ffn=True)
                        pp_t2 = [None, None]
                        for m in range(nm):
                            for n in range(2):
                                pp = (m * 2 + n) % 2
                                ps = self.PS[:, 1024 + pp * 512:1024 + (pp + 1) * 512]
                                wl = [t_wu, t_xn]
                                if pp_t2[pp]:
                                    wl.append(pp_t2[pp])
                                for k in range(16):
                                    t_mm = em.emit("tensor",
                                                   (lambda k, m, n, ps, uslot, wid: lambda t: t.matmul(
                                                       ps,
                                                       uslot[:, k * wid + m * 128:k * wid + (m + 1) * 128],
                                                       R1[:, k * 1024 + n * 512:k * 1024 + (n + 1) * 512],
                                                       start=(k == 0), stop=(k == 15)))(k, m, n, ps, uslot, wid),
                                                   wait=wl if k == 0 else (),
                                                   inc=self.pe if k == 15 else None)
                                last_umm = t_mm
                                gdst = R1[:, GOF + (sub0[sub] // 128 + m) * 1024 + n * 512:
                                          GOF + (sub0[sub] // 128 + m) * 1024 + (n + 1) * 512]
                                usrc = R1[:, USF + m * 1024 + n * 512:USF + m * 1024 + (n + 1) * 512]
                                t_g = em.emit("vector",
                                              (lambda gdst, usrc, ps: lambda v: v.tensor_mul(
                                                  gdst, usrc, ps))(gdst, usrc, ps),
                                              wait=[t_mm, sub_silu[m * 2 + n]],
                                              inc=self.ve)
                                g_toks.append(t_g)
                                pp_t2[pp] = t_g
                        wslot_tok[wsl2] = last_umm
                        prev_sub_gdone = g_toks[-1]
                    else:
                        g_toks.extend(sub_silu)
                        prev_sub_gdone = sub_silu[-1]
                wcw = [wc_tok] if wc_tok else []
                t_wc = em.dma("gpsimd", WC[:, 0:nslab * 2048].rearrange("p (k f) -> p k f", f=2048),
                              wd_ap.rearrange("(k p) f -> p k f", p=128), self.dW[2], wait=wcw)
                g_done = g_toks[-1]
                for mtq in range(8):
                    for nn in range(4):
                        pp = (mtq * 4 + nn) % 2
                        psd = [self.PA, self.PC][pp]
                        wl = [t_wc, g_done]
                        if ev_guard[pp]:
                            wl.append(ev_guard[pp])
                        for k in range(nslab):
                            t_dm = em.emit("tensor",
                                           (lambda k, mtq, nn, psd, nslab: lambda t: t.matmul(
                                               psd[:, :],
                                               R1[:, GOF + k * 1024 + mtq * 128:GOF + k * 1024 + mtq * 128 + 128],
                                               WC[:, k * 2048 + nn * 512:k * 2048 + (nn + 1) * 512],
                                               start=(k == 0), stop=(k == nslab - 1)))(k, mtq, nn, psd, nslab),
                                           wait=wl if k == 0 else (),
                                           inc=self.pe if k == nslab - 1 else None)
                        tile_idx = half * 8 + mtq
                        t_ev = em.emit("vector",
                                       (lambda mtq, nn, psd, tile_idx, ecol, stride: lambda v: v.scalar_tensor_tensor(
                                           ACC[:, mtq * 2048 + nn * 512:mtq * 2048 + (nn + 1) * 512],
                                           psd[:, :],
                                           self.CMB[:, tile_idx * stride + ecol:tile_idx * stride + ecol + 1],
                                           ACC[:, mtq * 2048 + nn * 512:mtq * 2048 + (nn + 1) * 512],
                                           op0=ALU.mult, op1=ALU.add))(mtq, nn, psd, tile_idx, ecol, stride),
                                       wait=[t_dm, t_acc, t_cmb], inc=self.ve)
                        ev_guard[pp] = t_ev
                wc_tok = t_dm
            xnh_guard = last_umm
            for q in range(8):
                tk = em.dma("sync", out_dst[(half * 8 + q) * 128:(half * 8 + q) * 128 + 128, :],
                            ACC[:, q * 2048:(q + 1) * 2048], self.dOUT,
                            wait=[ev_guard[0], ev_guard[1]] if q == 0 else ())
                acc_store_toks.append(tk)


    def p4(self):
        em, d = self.em, self.d
        XN, R2 = self.R1, self.R2
        TBL = self.CCR
        SLc = [self.CCR[:, 8192:10240], self.CCR[:, 10240:12288]]
        LNF = self.TSb[:, :]
        self._tt_guard = [None, None]
        self._wslot_tok = [None, None]

        t_tbl = em.dma("sync", TBL[:, 0:8192], self.tbl1_t.ap().bitcast(BF16), self.dST)
        t_rm = em.dma("sync", self.RMW[:, :], d["rmsw"], self.dST)
        self._brow_tok = {}

        t_tr = None
        for mt in range(MT):
            sl = mt % 2
            ts = self.TSa
            w = [self.dOUT] if mt >= 1 else []
            t_x = em.dma("sync", ts[:, :], d["xB"][mt * 128:(mt + 1) * 128, :], self.dX[0], wait=w)
            t_xn, t_ln = self.ln_tile(ts, t_x, 8 * sl, SLc[sl], rms=True, ln_f32_out=LNF)
            t_r = em.emit("vector", (lambda ts: lambda v: v.tensor_add(ts[:, :], ts[:, :], LNF[:, :]))(ts),
                          wait=[t_xn], inc=self.ve)
            t_rs = em.dma("sync", d["r_hbm"][mt * 128:(mt + 1) * 128, :], ts[:, :], self.dOUT, wait=[t_r])
            t_tr = self.transpose_tile(SLc[sl], mt, XN, t_xn)
        xn_tok = t_tr

        def proj_h(wdram, hq_base, tbl_q, boff, bias_vec):
            qsl_tok = [None, None]
            pp_tok = [None, None]
            hb_state = {}
            for u in range(4):
                wsl = u % 2
                w = [self._wslot_tok[wsl]] if self._wslot_tok[wsl] else []
                t_w, slot = self.dma_weight_unit(wsl, wdram, u * 512, 512, wait=w)
                t_row = self.bias_row(boff, bias_vec, u * 512, 512)
                t_w = self.rms_scale_unit(slot, 512, t_w, t_rm)
                t_last = None
                for m in range(4):
                    ch = u * 4 + m
                    qsl = SLc[ch % 2]
                    ch_evs = []
                    for n in range(4):
                        pp = (m * 4 + n) % 2
                        ps = [self.PA, self.PC][pp]
                        wl = [t_w, t_row, xn_tok, t_tbl, self.static_tok]
                        if pp_tok[pp]:
                            wl.append(pp_tok[pp])
                        if n == 0 and qsl_tok[ch % 2]:
                            wl.append(qsl_tok[ch % 2])
                        t_pre = em.emit("tensor",
                                (lambda m, ps, boff: lambda t: t.matmul(
                                    ps[:, :],
                                    self.BROW[0:1, boff + m * 128:boff + (m + 1) * 128],
                                    self.ONESB[0:1, 0:512],
                                    start=True, stop=False))(m, ps, boff),
                                wait=wl, inc=self.pe)
                        self._brow_tok[boff] = t_pre
                        for k in range(16):
                            t_mm = em.emit("tensor",
                                           (lambda k, m, n, ps, slot: lambda t: t.matmul(
                                               ps[:, :],
                                               slot[:, k * 512 + m * 128:k * 512 + (m + 1) * 128],
                                               XN[:, k * 2048 + n * 512:k * 2048 + (n + 1) * 512],
                                               start=False, stop=(k == 15)))(k, m, n, ps, slot),
                                           inc=self.pe if k == 15 else None)
                        t_ev = em.emit("scalar",
                                       (lambda ch, n, ps, qsl: lambda s: s.activation(
                                           qsl[:, n * 512:(n + 1) * 512], ps[:, :], AF.Copy))(ch, n, ps, qsl),
                                       wait=[t_mm], inc=self.se)
                        pp_tok[pp] = t_ev
                        ch_evs.append(t_ev)
                        t_last = t_mm
                    t_hb = self.hbuild(qsl, ch, hq_base, tbl_q, ch_evs, hb_state)
                    qsl_tok[ch % 2] = t_hb
                self._wslot_tok[wsl] = t_last
        proj_h(d["wq1"], 0, 0, 0, d["bq1"])
        proj_h(d["wk1"], 16384, 4096, 512, d["bk1"])

        vsl_tok = [None, None]
        pp_tok = [None, None]
        for u in range(4):
            wsl = u % 2
            w = [self._wslot_tok[wsl]] if self._wslot_tok[wsl] else []
            t_w, vslot = self.dma_weight_unit(wsl, d["wv1"], u * 512, 512, wait=w)
            t_bv = self.bias_row(1024, d["bv1"], u * 512, 512)
            t_w = self.rms_scale_unit(vslot, 512, t_w, t_rm)
            t_last = None
            for mt in range(MT):
                pp = mt % 2
                ps = [self.PA, self.PC][pp]
                wl = [t_w, xn_tok, t_bv, self.static_tok]
                if pp_tok[pp]:
                    wl.append(pp_tok[pp])
                t_pre = em.emit("tensor",
                        (lambda u, ps: lambda t: t.matmul(
                            ps[:, :], self.ONESB[0:1, 0:128],
                            self.BROW[0:1, 1024:1536],
                            start=True, stop=False))(u, ps),
                        wait=wl, inc=self.pe)
                self._brow_tok[1024] = t_pre
                for k in range(16):
                    t_mm = em.emit("tensor",
                                   (lambda k, mt, ps, vslot: lambda t: t.matmul(
                                       ps[:, :], XN[:, k * 2048 + mt * 128:k * 2048 + mt * 128 + 128],
                                       vslot[:, k * 512:(k + 1) * 512],
                                       start=False, stop=(k == 15)))(k, mt, ps, vslot),
                                   inc=self.pe if k == 15 else None)
                vsl = SLc[pp]
                wl2 = [t_mm]
                if vsl_tok[pp]:
                    wl2.append(vsl_tok[pp])
                t_ev = em.emit("scalar",
                               (lambda vsl, ps: lambda s: s.activation(
                                   vsl[:, 0:512], ps[:, :], AF.Copy))(vsl, ps),
                               wait=wl2, inc=self.se)
                t_dma = em.dma("sync", d["v_hbm"][mt * 128:(mt + 1) * 128, u * 512:(u + 1) * 512],
                               vsl[:, 0:512], self.dS[pp], wait=[t_ev])
                vsl_tok[pp] = t_dma
                pp_tok[pp] = t_ev
                t_last = t_mm
            self._wslot_tok[wsl] = t_last

    def hbuild(self, qsl, ch, hq_base, tbl_q, ch_evs, state):
        em = self.em
        TBL = self.CCR
        R2 = self.R2
        rr = 64 * (ch % 2)
        hc = ch // 2
        to_off = tbl_q if ch % 2 == 0 else tbl_q + 2048
        tw_off = tbl_q + 2048 if ch % 2 == 0 else tbl_q
        last = None
        for n in range(4):
            TMP = self.TMPF[:, (n % 2) * 512:(n % 2) * 512 + 512]
            g = state.get(("tmp", n % 2))
            t_sw = em.emit("tensor",
                           (lambda n, qsl: lambda t: t.matmul(
                               self.PS[:, 0:512], self.PERM1[:, :],
                               qsl[:, n * 512:(n + 1) * 512],
                               start=True, stop=True))(n, qsl),
                           wait=[ch_evs[n], self.static_tok] + ([state["pd"]] if "pd" in state else []), inc=self.pe)
            t1 = em.emit("vector",
                         (lambda n, TMP, rr, tw_off: lambda v: v.tensor_mul(
                             TMP[rr:rr + 64, :], self.PS[rr:rr + 64, 0:512],
                             TBL[rr:rr + 64, tw_off + n * 512:tw_off + (n + 1) * 512]))(n, TMP, rr, tw_off),
                         wait=[t_sw] + ([g] if g else []), inc=self.ve)
            state["pd"] = t1
            t2 = em.emit("vector",
                         (lambda n, qsl, rr, hc, to_off: lambda v: v.tensor_mul(
                             R2[rr:rr + 64, hq_base + hc * 2048 + n * 512:hq_base + hc * 2048 + (n + 1) * 512],
                             qsl[rr:rr + 64, n * 512:(n + 1) * 512],
                             TBL[rr:rr + 64, to_off + n * 512:to_off + (n + 1) * 512]))(n, qsl, rr, hc, to_off),
                         wait=[ch_evs[n]] + ([state.get(("h", hc, rr, n))] if state.get(("h", hc, rr, n)) else []),
                         inc=self.ve)
            t3 = em.emit("vector",
                         (lambda n, rr, hc, TMP: lambda v: v.tensor_add(
                             R2[rr:rr + 64, hq_base + hc * 2048 + n * 512:hq_base + hc * 2048 + (n + 1) * 512],
                             R2[rr:rr + 64, hq_base + hc * 2048 + n * 512:hq_base + hc * 2048 + (n + 1) * 512],
                             TMP[rr:rr + 64, :]))(n, rr, hc, TMP),
                         wait=[t1, t2], inc=self.ve)
            state[("tmp", n % 2)] = t3
            last = t3
        return last


    def p5(self):
        em, d = self.em, self.d
        t_v = em.dma("sync", self.R1[:, :].rearrange("p (j f) -> p j f", f=2048),
                     d["v_hbm"].rearrange("(j p) f -> p j f", p=128), self.dR1)
        self.attention(layer=1, t_kv=[t_v], QD=self.R2, KD=self.R2, R1=self.R1)


    def build(self):
        em = self.em
        em.dma("sync", self.IDT[:, :], self.ident_t.ap().bitcast(BF16), self.dST)
        em.dma("sync", self.ONESB[0:1, :], self.ones_t.ap().bitcast(BF16), self.dST)
        em.dma("sync", self.PERM[:, :], self.perm_t.ap().bitcast(BF16), self.dST)
        em.dma("sync", self.PERM1[:, :], self.perm1_t.ap().bitcast(BF16), self.dST)
        em.dma("sync", self.MASK[:, :], self.mask_t.ap(), self.dST)
        self.static_tok = self.dST
        fns = {0: self.p0, 1: self.p1, 2: lambda: self.p_out(0), 3: lambda: self.p_ffn(0),
               4: self.p4, 5: self.p5, 6: lambda: self.p_out(1), 7: lambda: self.p_ffn(1)}
        self._wslot_tok = [None, None]
        for ph in self.phases:
            fns[ph]()
            self.barrier()
        em.check()
        em.replay()
        return self.nc



def host_prep(inputs):
    f = lambda k: np.ascontiguousarray(np.asarray(inputs[k], dtype=np.float32))
    pc = lambda a: np.ascontiguousarray(a.reshape(16, 128).T)
    w = {}
    w["wq0"] = f("L0_attn_Wq"); w["bq0"] = f("L0_attn_bq")
    w["wk0"] = f("L0_attn_Wk")
    w["bk0"] = f("L0_attn_bk")
    w["wv0"] = f("L0_attn_Wv"); w["bv0"] = f("L0_attn_bv")
    w["wo0"] = f("L0_attn_Wo"); w["bo0"] = f("L0_attn_bo")
    w["gw0"] = f("L0_ffn_gate_w")
    w["wg0"] = f("L0_ffn_w_gate"); w["wu0"] = f("L0_ffn_w_up"); w["wd0"] = f("L0_ffn_w_down")
    w["swg0"] = f("L0_ffn_sw_gate"); w["swu0"] = f("L0_ffn_sw_up"); w["swd0"] = f("L0_ffn_sw_down")
    perm = l1_head_perm()
    w["wq1"] = np.ascontiguousarray(f("L1_attn_wq")[:, perm])
    w["bq1"] = np.ascontiguousarray(f("L1_attn_bq")[perm])
    w["wk1"] = np.ascontiguousarray(f("L1_attn_wk")[:, perm])
    w["bk1"] = np.ascontiguousarray(f("L1_attn_bk")[perm])
    w["wv1"] = f("L1_attn_wv"); w["bv1"] = f("L1_attn_bv")
    w["wo1"] = f("L1_attn_wo"); w["bo1"] = f("L1_attn_bo")
    w["gw1"] = np.ascontiguousarray(f("L1_ffn_gate_w").T)
    w["rmsw"] = pc(f("L1_attn_rms_w"))
    w["w1"] = f("L1_ffn_w1"); w["w2"] = f("L1_ffn_w2")
    w["sw1"] = np.ascontiguousarray(f("L1_ffn_sw1")[0])
    w["sw2"] = np.ascontiguousarray(f("L1_ffn_sw2")[0])
    return w



from concourse.bass_utils import run_bass_kernel_spmd

_PROG_CACHE = {}


def _get_prog(group):
    if group not in _PROG_CACHE:
        m = MK(group)
        nc = m.build()
        _PROG_CACHE[group] = (m, nc)
    return _PROG_CACHE[group]


def kernel(**inputs):
    w = host_prep(inputs)
    x_all = np.ascontiguousarray(np.asarray(inputs["L0_attn_x"], dtype=np.float32))
    ncores = 8
    m, nc = _get_prog("ALL")
    in_maps = []
    for c in range(ncores):
        dmap = {}
        for n in m.ext_in:
            if n == "x":
                dmap[n] = np.ascontiguousarray(x_all[c])
            else:
                dmap[n] = w[n]
        in_maps.append(dmap)
    res = run_bass_kernel_spmd(nc, in_maps, core_ids=list(range(ncores)))
    out = np.stack([np.asarray(res.results[c]["out"], dtype=np.float32) for c in range(ncores)], axis=0)
    return out
